# version 01
# baseline (speedup 1.0000x reference)
import math
import numpy as np
import ml_dtypes

import concourse.bass as bass
import concourse.mybir as mybir
from concourse.bass_utils import run_bass_kernel_spmd

F32 = mybir.dt.float32
BF16 = mybir.dt.bfloat16
AF = mybir.ActivationFunctionType
AL = mybir.AluOpType

B, S, D = 4, 4096, 2048
NH, HD = 16, 128
NFF = 8192
KTOP = 819
NCORES = 8
OWN = S // 2
NBLK = S // 128
NOWN = OWN // 128
INV_SQRT_HD = 1.0 / math.sqrt(128.0)
LN_EPS = 1e-5
NEG = -1.0e30
J0 = 13
P3_HEADS = NH
P3_WO = True
DBG_ATT = False

ENGINES = ("sync", "scalar", "vector", "tensor", "gpsimd")


class Plan:

    def __init__(self, nc, sem_names):
        self.nc = nc
        self.ops = {e: [] for e in ENGINES}
        self.sems = {}
        self.val = {}
        self.ctxs = []
        for name in sem_names:
            ctx = nc.semaphore(name)
            self.sems[name] = ctx.__enter__()
            self.ctxs.append(ctx)
            self.val[name] = 0

    def do(self, eng, fn, inc=None, n=1):
        if inc is None:
            self.ops[eng].append(("op", fn, None, 0))
            return None
        self.val[inc] += n
        self.ops[eng].append(("op", fn, self.sems[inc], n))
        return self.val[inc]

    def dma(self, eng, fn, inc="ld"):
        return self.do(eng, fn, inc, 16)

    def wait(self, eng, sem, val):
        if val is None or val <= 0:
            return
        self.ops[eng].append(("wait", None, self.sems[sem], val))

    def cur(self, sem):
        return self.val[sem]

    def replay(self, block):
        def runner(ops):
            def _(eng):
                for kind, fn, sem, n in ops:
                    if kind == "wait":
                        eng.wait_ge(sem, n)
                    else:
                        instr = fn(eng)
                        if sem is not None:
                            instr.then_inc(sem, n)
            return _

        block.sync(runner(self.ops["sync"]))
        block.scalar(runner(self.ops["scalar"]))
        block.vector(runner(self.ops["vector"]))
        block.tensor(runner(self.ops["tensor"]))
        block.gpsimd(runner(self.ops["gpsimd"]))


class PsumPool:

    def __init__(self, plan, banks, consumer_sem="v"):
        self.p = plan
        self.banks = list(banks)
        self.sem = consumer_sem
        self.release = [0] * len(self.banks)
        self.idx = 0

    def acquire(self):
        i = self.idx % len(self.banks)
        self.idx += 1
        self.p.wait("tensor", self.sem, self.release[i])
        return i, self.banks[i]

    def released(self, i, val):
        self.release[i] = val


def host_masks(par):
    masks = np.zeros((9, 128, 512), dtype=np.float32)
    for r in range(9):
        for s4 in range(4):
            off = (r - 2 * s4 - par) * 128
            for rr in range(128):
                lim = off + rr
                if lim <= 0:
                    continue
                masks[r, rr, s4 * 128: s4 * 128 + min(lim, 128)] = NEG
    return masks


class Ctx:
    pass


def declare(nc, debug_taps=()):
    c = Ctx()
    dt_ = nc.dram_tensor
    c.x_seq = dt_("x_seq", [S, D], F32, kind="ExternalInput").ap()
    c.x_own = dt_("x_own", [OWN, D], F32, kind="ExternalInput").ap()
    c.cmask = dt_("cmask", [9, 128, 512], F32, kind="ExternalInput").ap()
    c.w_qkv = dt_("w_qkv", [3 * D, D], F32, kind="ExternalInput").ap()
    c.w_o = dt_("w_o", [D, D], F32, kind="ExternalInput").ap()
    c.W_syn = dt_("W_syn", [NFF, D], F32, kind="ExternalInput").ap()
    c.b_syn = dt_("b_syn", [NFF], F32, kind="ExternalInput").ap()
    c.W_out = dt_("W_out", [D, NFF], F32, kind="ExternalInput").ap()
    c.b_out = dt_("b_out", [D], F32, kind="ExternalInput").ap()
    c.gamma = dt_("gamma", [D], F32, kind="ExternalInput").ap()
    c.beta = dt_("beta", [D], F32, kind="ExternalInput").ap()

    c.y = dt_("y", [OWN, D], F32, kind="ExternalOutput").ap()

    def scratch(name, shape, dtype):
        if name in debug_taps:
            return dt_(name, shape, dtype, kind="ExternalOutput").ap()
        return dt_(name, shape, dtype).ap()

    c.xn_tok = scratch("xn_tok", [S, D], BF16)
    c.xn_own = scratch("xn_own", [OWN, D], BF16)
    c.qf = scratch("qf", [D, OWN], BF16)
    c.kf = scratch("kf", [D, S], BF16)
    c.vt = scratch("vt", [NH, S, HD], BF16)
    c.wqkv_bf = scratch("wqkv_bf", [3 * D, D], BF16)
    c.wo_bf = scratch("wo_bf", [D, D], BF16)
    c.tsyn = scratch("tsyn", [NFF, D], BF16)
    c.tout = scratch("tout", [D, NFF], BF16)
    c.bsyn128 = scratch("bsyn128", [128, NFF], F32)
    c.cst128 = scratch("cst128", [3, 128, D], F32)
    c.scal128 = scratch("scal128", [128, 4], F32)
    c.dbg_small = scratch("dbg_small", [128, 1024], F32)
    c.dbg_att = scratch("dbg_att", [128, 2048], F32)
    c.z_sp = scratch("z_sp", [OWN, NFF], F32)
    c.h_sp = scratch("h_sp", [OWN, NFF], BF16)
    c.g_sp = scratch("g_sp", [OWN, D], F32)

    c.ones_row_f = nc.inline_tensor(np.ones((1, 128), dtype=np.float32),
                                    name="ones_row_f").ap()
    c.ones_col_f = nc.inline_tensor(np.ones((128, 1), dtype=np.float32),
                                    name="ones_col_f").ap()
    c.ones_col_bf = nc.inline_tensor(np.ones((128, 1), dtype=ml_dtypes.bfloat16),
                                     name="ones_col_bf").ap()
    iota16 = np.broadcast_to(np.arange(16, dtype=np.float32), (128, 16)).copy()
    c.iota16 = nc.inline_tensor(iota16, name="iota16").ap()
    konst = np.broadcast_to(np.array([-0.5, 1.0, 0.0, 0.0], dtype=np.float32),
                            (128, 4)).copy()
    c.konst = nc.inline_tensor(konst, name="konst").ap()
    return c



def phase0(p, c, m):
    xt = [m.bufB[:, 0:2048], m.bufB[:, 2048:4096]]
    junk = m.bufA[:, 0:2048]
    xnt = [m.bufA[:, 2048:4096], m.bufA[:, 4096:6144]]
    sm = m.small
    st_sums, st_sums2, st_m, st_t1, st_t2, st_r, st_d = (
        sm[:, 0:1], sm[:, 1:2], sm[:, 2:3], sm[:, 3:4], sm[:, 4:5], sm[:, 5:6],
        sm[:, 6:7])

    p.do("vector", lambda e: e.memset(sm[:, :], 0.0))
    p.do("vector", lambda e: e.memset(m.bufC[:, :], 0.0))

    state = {"i": 0, "st": [0, 0]}

    def ln1_tile(src_rows, dst):
        i = state["i"]; state["i"] += 1
        xb, xo = xt[i % 2], xnt[i % 2]
        lv = p.dma("sync", lambda e, xb=xb, src=src_rows: e.dma_start(out=xb, in_=src))
        p.wait("vector", "ld", lv)
        p.wait("vector", "st", state["st"][i % 2])
        p.do("vector", lambda e, xb=xb: e.tensor_scalar(
            out=junk, in0=xb, scalar1=0.0, scalar2=None, op0=AL.add, op1=AL.add,
            accum_out=st_sums))
        p.do("vector", lambda e, xb=xb: e.scalar_tensor_tensor(
            out=junk, in0=xb, scalar=0.0, in1=xb, op0=AL.add, op1=AL.mult,
            accum_out=st_sums2))
        p.do("vector", lambda e: e.tensor_scalar(
            out=st_m, in0=st_sums, scalar1=1.0 / D, scalar2=None, op0=AL.mult))
        p.do("vector", lambda e: e.tensor_scalar(
            out=st_t1, in0=st_sums2, scalar1=1.0 / D, scalar2=None, op0=AL.mult))
        p.do("vector", lambda e: e.scalar_tensor_tensor(
            out=st_t2, in0=st_m, scalar=-1.0, in1=st_m, op0=AL.mult, op1=AL.mult))
        p.do("vector", lambda e: e.tensor_scalar(
            out=st_t1, in0=st_t1, scalar1=LN_EPS, scalar2=None, op0=AL.add))
        p.do("vector", lambda e: e.tensor_copy(out=junk[:, 0:1], in_=st_m))
        vv = p.do("vector", lambda e: e.tensor_tensor(
            out=st_d, in0=st_t1, in1=st_t2, op=AL.add), "v")
        p.wait("scalar", "v", vv)
        p.do("scalar", lambda e: e.activation(out=st_t1, in_=st_d, func=AF.Ln))
        p.do("scalar", lambda e: e.activation(out=st_t2, in_=st_d, func=AF.Ln))
        sv = p.do("scalar", lambda e: e.activation(
            out=st_r, in_=st_t1, func=AF.Exp, scale=-0.5), "s")
        p.wait("vector", "s", sv)
        p.do("vector", lambda e: e.scalar_tensor_tensor(
            out=st_t1, in0=st_r, scalar=1.0, in1=st_r, op0=AL.mult, op1=AL.mult))
        p.do("vector", lambda e: e.tensor_copy(out=junk[:, 0:1], in_=st_m))
        p.do("vector", lambda e: e.tensor_copy(out=junk[:, 8:9], in_=st_m))
        p.do("vector", lambda e: e.scalar_tensor_tensor(
            out=st_t1, in0=st_t1, scalar=-0.5, in1=st_d, op0=AL.mult, op1=AL.mult))
        p.do("vector", lambda e: e.tensor_copy(out=junk[:, 0:1], in_=st_m))
        p.do("vector", lambda e: e.tensor_copy(out=junk[:, 8:9], in_=st_m))
        p.do("vector", lambda e: e.tensor_scalar(
            out=st_t1, in0=st_t1, scalar1=1.5, scalar2=None, op0=AL.add))
        p.do("vector", lambda e: e.tensor_copy(out=junk[:, 0:1], in_=st_m))
        p.do("vector", lambda e: e.tensor_copy(out=junk[:, 8:9], in_=st_m))
        p.do("vector", lambda e: e.tensor_tensor(out=st_r, in0=st_r, in1=st_t1, op=AL.mult))
        p.do("vector", lambda e: e.tensor_copy(out=junk[:, 0:1], in_=st_m))
        p.do("vector", lambda e: e.tensor_copy(out=junk[:, 8:9], in_=st_m))
        wv = p.do("vector", lambda e, xb=xb, xo=xo: e.tensor_scalar(
            out=xo, in0=xb, scalar1=st_m, scalar2=st_r, op0=AL.subtract, op1=AL.mult), "v")
        p.wait("sync", "v", wv)
        state["st"][i % 2] = p.dma(
            "sync", lambda e, xo=xo, dst=dst: e.dma_start(out=dst, in_=xo), "st")

    for i in range(NBLK):
        ln1_tile(c.x_seq[i * 128:(i + 1) * 128, :], c.xn_tok[i * 128:(i + 1) * 128, :])
    for i in range(NOWN):
        ln1_tile(c.x_own[i * 128:(i + 1) * 128, :], c.xn_own[i * 128:(i + 1) * 128, :])

    brow = m.bufB[0:1, 4096:4096 + NFF]
    ones_s = m.small[0:1, 160:288]
    onescol_s = sm[:, 288:289]
    lv0 = p.dma("sync", lambda e: e.dma_start(out=ones_s, in_=c.ones_row_f))
    lv1 = p.dma("sync", lambda e: e.dma_start(out=onescol_s, in_=c.ones_col_f))
    pool = PsumPool(p, m.psb[0:4])

    def bcast_row(vec_dram, n, out_dram, off):
        lv = p.dma("sync", lambda e, n=n, v=vec_dram: e.dma_start(
            out=brow[:, 0:n], in_=v.unsqueeze(0)))
        p.wait("tensor", "ld", max(lv, lv0))
        last = None
        for ci in range(n // 512):
            bi, bank = pool.acquire()
            tv = p.do("tensor", lambda e, ci=ci, bank=bank: e.matmul(
                bank[:, :], ones_s, brow[:, ci * 512:(ci + 1) * 512],
                start=True, stop=True), "t")
            p.wait("vector", "t", tv)
            last = p.do("vector", lambda e, bank=bank, ci=ci, off=off: e.tensor_copy(
                out=m.bufC[:, off + ci * 512: off + (ci + 1) * 512], in_=bank[:, :]), "v")
            pool.released(bi, last)
        p.wait("sync", "v", last)
        return p.dma("sync", lambda e, n=n, o=out_dram, off=off: e.dma_start(
            out=o, in_=m.bufC[:, off:off + n]), "st")

    st_bsyn = bcast_row(c.b_syn, NFF, c.bsyn128[:, :], 0)
    p.wait("vector", "st", st_bsyn)
    bcast_row(c.b_out, D, c.cst128[0], 0)
    bcast_row(c.gamma, D, c.cst128[1], 2048)
    bcast_row(c.beta, D, c.cst128[2], 4096)

    absacc = sm[:, 16:144]
    sink2k = m.bufA[:, 6144:8192]
    sink8k = m.bufA[:, 8192:16384]
    wt2k = [m.bufB[:, 0:2048], m.bufB[:, 2048:4096]]
    col = 0
    s_done = [0, 0]
    last_s = None
    for i in range(NFF // 128):
        wb = wt2k[i % 2]
        p.wait("sync", "s", s_done[i % 2])
        lv = p.dma("sync", lambda e, i=i, wb=wb: e.dma_start(
            out=wb, in_=c.W_syn[i * 128:(i + 1) * 128, :]))
        p.wait("scalar", "ld", lv)
        last_s = p.do("scalar", lambda e, wb=wb, col=col: e.activation(
            out=sink2k, in_=wb, func=AF.Abs, accum_out=absacc[:, col:col + 1]), "s")
        s_done[i % 2] = last_s
        col += 1
    p.wait("vector", "s", last_s)
    p.do("vector", lambda e: e.tensor_scalar(
        out=junk[:, 0:64], in0=absacc[:, 0:64], scalar1=0.0, scalar2=None,
        op0=AL.add, op1=AL.add, accum_out=sm[:, 8:9]))

    wt8k = [m.bufB[:, 0:8192], m.bufB[:, 8192:16384]]
    s_done = [last_s, last_s]
    for i in range(D // 128):
        wb = wt8k[i % 2]
        p.wait("sync", "s", s_done[i % 2])
        lv = p.dma("sync", lambda e, i=i, wb=wb: e.dma_start(
            out=wb, in_=c.W_out[i * 128:(i + 1) * 128, :]))
        p.wait("scalar", "ld", lv)
        last_s = p.do("scalar", lambda e, wb=wb, col=col: e.activation(
            out=sink8k, in_=wb, func=AF.Abs, accum_out=absacc[:, col:col + 1]), "s")
        s_done[i % 2] = last_s
        col += 1
    p.wait("vector", "s", last_s)
    vv = p.do("vector", lambda e: e.tensor_scalar(
        out=junk[:, 0:16], in0=absacc[:, 64:80], scalar1=0.0, scalar2=None,
        op0=AL.add, op1=AL.add, accum_out=sm[:, 9:10]), "v")

    p.wait("tensor", "v", vv)
    p.wait("tensor", "ld", lv1)
    bank = m.psb[6]
    p.do("tensor", lambda e, bank=bank: e.matmul(
        bank[0:1, 0:1], sm[:, 8:9], onescol_s, start=True, stop=False))
    tv = p.do("tensor", lambda e, bank=bank: e.matmul(
        bank[0:1, 1:2], sm[:, 9:10], onescol_s, start=False, stop=True), "t")
    p.wait("vector", "t", tv)
    srow = sm[0:1, 144:152]
    p.do("vector", lambda e, bank=bank: e.tensor_scalar(
        out=srow[:, 0:2], in0=bank[0:1, 0:2], scalar1=0.5 / (NFF * D),
        scalar2=None, op0=AL.mult))
    p.do("vector", lambda e: e.tensor_copy(out=junk[:, 0:1], in_=sm[:, 0:1]))
    p.do("vector", lambda e: e.tensor_copy(out=junk[:, 8:9], in_=sm[:, 0:1]))
    vv = p.do("vector", lambda e: e.tensor_scalar(
        out=srow[:, 2:4], in0=srow[:, 0:2], scalar1=0.5e-8, scalar2=None,
        op0=AL.add), "v")
    p.wait("tensor", "v", vv)
    tv = p.do("tensor", lambda e, bank=bank: e.matmul(
        bank[:, 4:8], ones_s, srow[:, 0:4], start=True, stop=True), "t")
    p.wait("vector", "t", tv)
    vv = p.do("vector", lambda e, bank=bank: e.tensor_copy(
        out=sm[:, 148:152], in_=bank[:, 4:8]), "v")
    p.wait("sync", "v", vv)
    p.dma("sync", lambda e: e.dma_start(out=c.scal128[:, :], in_=sm[:, 148:152]), "st")
    p.dma("sync", lambda e: e.dma_start(out=c.dbg_small[:, :], in_=sm[:, 0:1024]), "st")
    p.wait("sync", "st", p.cur("st"))


def build_program(debug_taps=(), phases=("p0",)):
    nc = bass.Bass("TRN2", target_bir_lowering=False, debug=False)
    c = declare(nc, debug_taps)
    import contextlib
    with contextlib.ExitStack() as stack:
        m = Ctx()
        m.bufA = stack.enter_context(nc.sbuf_tensor("bufA", [128, 49152], BF16))
        m.bufB = stack.enter_context(nc.sbuf_tensor("bufB", [128, 16384], F32))
        m.bufC = stack.enter_context(nc.sbuf_tensor("bufC", [128, 8192], F32))
        m.small = stack.enter_context(nc.sbuf_tensor("small", [128, 3072], F32))
        m.psb = [stack.enter_context(nc.psum_tensor(f"pb{i}", [128, 512], F32))
                 for i in range(8)]
        p = Plan(nc, ["ld", "lds", "ldg", "t", "v", "s", "g", "st"])
        if "p0" in phases:
            phase0(p, c, m)
        if "p1" in phases:
            phase1(p, c, m)
        if "p2" in phases:
            phase2(p, c, m)
        if "p3" in phases:
            phase3(p, c, m)
        with nc.Block() as block:
            p.replay(block)
        for ctx in p.ctxs:
            ctx.__exit__(None, None, None)
    return nc


def phase1(p, c, m):
    p.wait("gpsimd", "v", p.cur("v"))
    p.wait("gpsimd", "s", p.cur("s"))
    p.wait("scalar", "st", p.cur("st"))
    castbuf = [m.bufA[:, 0:4096], m.bufA[:, 4096:8192]]
    g_done = [0, 0]
    for i in range(48):
        cb = castbuf[i % 2][:, 0:2048]
        p.wait("gpsimd", "ldg", g_done[i % 2])
        lv = p.do("gpsimd", lambda e, i=i, cb=cb: e.dma_start(
            out=cb, in_=c.w_qkv[i * 128:(i + 1) * 128, :]), "ldg", 16)
        p.wait("gpsimd", "ldg", lv)
        g_done[i % 2] = p.do("gpsimd", lambda e, i=i, cb=cb: e.dma_start(
            out=c.wqkv_bf[i * 128:(i + 1) * 128, :], in_=cb), "ldg", 16)
    for i in range(16):
        cb = castbuf[i % 2][:, 0:2048]
        p.wait("gpsimd", "ldg", g_done[i % 2])
        lv = p.do("gpsimd", lambda e, i=i, cb=cb: e.dma_start(
            out=cb, in_=c.w_o[i * 128:(i + 1) * 128, :]), "ldg", 16)
        p.wait("gpsimd", "ldg", lv)
        g_done[i % 2] = p.do("gpsimd", lambda e, i=i, cb=cb: e.dma_start(
            out=c.wo_bf[i * 128:(i + 1) * 128, :], in_=cb), "ldg", 16)
    wq_ready = g_done[0] if g_done[0] > g_done[1] else g_done[1]

    wt = [m.bufA[:, i * 2048:(i + 1) * 2048] for i in range(16)]
    xnf_all = m.bufB.bitcast(BF16)
    xnf = [xnf_all[:, i * 2048:(i + 1) * 2048] for i in range(16)]
    stage = [m.bufA[:, 32768 + i * 512: 32768 + (i + 1) * 512] for i in range(8)]
    stage_rel = [0] * 8
    pool = PsumPool(p, m.psb, consumer_sem="s")

    def load_weights(row0):
        vals = []
        for dc in range(16):
            lv = p.do("scalar", lambda e, dc=dc, row0=row0: e.dma_start(
                out=wt[dc], in_=c.wqkv_bf[row0:row0 + 2048, dc * 128:(dc + 1) * 128],
                transpose=True), "lds", 16)
            vals.append(lv)
        return vals

    def load_xnf(src, t0):
        vals = []
        for dc in range(16):
            lv = p.do("scalar", lambda e, dc=dc, src=src, t0=t0: e.dma_start(
                out=xnf[dc], in_=src[t0:t0 + 2048, dc * 128:(dc + 1) * 128],
                transpose=True), "lds", 16)
            vals.append(lv)
        return vals

    def run_fpass(weight_row0, xn_src, xn_len, store_fn, pre_wait_ldg=0):
        if pre_wait_ldg:
            p.wait("scalar", "ldg", pre_wait_ldg)
        wv = load_weights(weight_row0)
        for tg in range(xn_len // 2048):
            xv = load_xnf(xn_src, tg * 2048)
            units = [(oc, tc) for oc in range(16) for tc in range(4)]
            for u0 in range(0, len(units), 2):
                pair = units[u0:u0 + 2]
                acq = [pool.acquire() for _ in pair]
                tvals = [None] * len(pair)
                p.wait("tensor", "lds", max(max(wv), max(xv)))
                for dc in range(16):
                    for j, (oc, tc) in enumerate(pair):
                        bi, bank = acq[j]
                        inc = "t" if dc == 15 else None
                        v = p.do("tensor",
                            lambda e, dc=dc, oc=oc, tc=tc, bank=bank: e.matmul(
                                bank[:, :], wt[dc][:, oc * 128:(oc + 1) * 128],
                                xnf[dc][:, tc * 512:(tc + 1) * 512],
                                start=(dc == 0), stop=(dc == 15)), inc)
                        if v is not None:
                            tvals[j] = v
                for j, (oc, tc) in enumerate(pair):
                    bi, bank = acq[j]
                    p.wait("scalar", "t", tvals[j])
                    p.wait("scalar", "st", stage_rel[bi])
                    stg = stage[bi]
                    sv = p.do("scalar", lambda e, bank=bank, stg=stg: e.activation(
                        out=stg, in_=bank[:, :], func=AF.Copy), "s")
                    pool.released(bi, sv)
                    p.wait("sync", "s", sv)
                    store_fn(oc, tg, tc, stg)
                    stage_rel[bi] = p.cur("st")

    def q_store(oc, tg, tc, stg):
        p.dma("sync", lambda e, oc=oc, tc=tc, stg=stg: e.dma_start(
            out=c.qf[oc * 128:(oc + 1) * 128, tc * 512:(tc + 1) * 512], in_=stg), "st")

    def k_store(oc, tg, tc, stg):
        col = tg * 2048 + tc * 512
        p.dma("sync", lambda e, oc=oc, col=col, stg=stg: e.dma_start(
            out=c.kf[oc * 128:(oc + 1) * 128, col:col + 512], in_=stg), "st")

    run_fpass(0, c.xn_own, OWN, q_store, pre_wait_ldg=p.cur("ldg"))
    run_fpass(2048, c.xn_tok, S, k_store)

    wv = load_weights(4096)
    for tg in range(2):
        xv = load_xnf(c.xn_tok, tg * 2048)
        units = [(tk, vo) for tk in range(16) for vo in range(4)]
        for u0 in range(0, len(units), 2):
            pair = units[u0:u0 + 2]
            acq = [pool.acquire() for _ in pair]
            tvals = [None] * len(pair)
            p.wait("tensor", "lds", max(max(wv), max(xv)))
            for dc in range(16):
                for j, (tk, vo) in enumerate(pair):
                    bi, bank = acq[j]
                    inc = "t" if dc == 15 else None
                    v = p.do("tensor",
                        lambda e, dc=dc, tk=tk, vo=vo, bank=bank: e.matmul(
                            bank[:, :], xnf[dc][:, tk * 128:(tk + 1) * 128],
                            wt[dc][:, vo * 512:(vo + 1) * 512],
                            start=(dc == 0), stop=(dc == 15)), inc)
                    if v is not None:
                        tvals[j] = v
            for j, (tk, vo) in enumerate(pair):
                bi, bank = acq[j]
                tv = tvals[j]
                p.wait("scalar", "t", tv)
                p.wait("scalar", "st", stage_rel[bi])
                stg = stage[bi]
                sv = p.do("scalar", lambda e, bank=bank, stg=stg: e.activation(
                    out=stg, in_=bank[:, :], func=AF.Copy), "s")
                pool.released(bi, sv)
                p.wait("sync", "s", sv)
                t0 = tg * 2048 + tk * 128
                p.dma("sync", lambda e, t0=t0, vo=vo, stg=stg: e.dma_start(
                    out=c.vt[vo * 4:(vo + 1) * 4, t0:t0 + 128, :].transpose([1, 0, 2]),
                    in_=stg.rearrange("p (h d) -> p h d", h=4)), "st")
                stage_rel[bi] = p.cur("st")
    p.wait("sync", "st", p.cur("st"))


def phase2(p, c, m):
    sm = m.small
    ST = {n: sm[:, i:i + 1] for i, n in enumerate(
        ["lo", "hi", "chi", "negv", "cg", "v", "pred", "t1", "t2", "sz", "sz2",
         "mu", "var", "sb", "neglo", "vstar", "cnt", "gap"])}
    iota16 = sm[:, 32:48]
    konst = sm[:, 48:52]
    scalv = sm[:, 52:56]
    top16 = sm[:, 56:72]
    oh16 = sm[:, 72:88]
    negtau = sm[:, 88:90]
    junk2 = sm[:, 90:92]
    bsl = sm[:, 1024:2048]

    lv1 = p.dma("sync", lambda e: e.dma_start(out=iota16, in_=c.iota16))
    lv2 = p.dma("sync", lambda e: e.dma_start(out=konst, in_=c.konst))
    lv3 = p.dma("sync", lambda e: e.dma_start(out=scalv, in_=c.scal128[:, :]))
    p.wait("vector", "ld", max(lv1, lv2, lv3))
    p.do("vector", lambda e: e.tensor_scalar(
        out=negtau, in0=scalv[:, 2:4], scalar1=-1.0, scalar2=None, op0=AL.mult))

    wbufs = [m.bufB[:, 0:8192], m.bufB[:, 8192:16384]]
    sg1 = m.bufA[:, 0:8192]
    sg2 = m.bufA[:, 8192:16384]
    tstage = m.bufA[:, 16384:24576]
    vrel = [0, 0]
    srel = [0, 0]

    def tbuild(tiles, src_fn, dst_fn, width, tau_col):
        stv = 0
        for i in range(tiles):
            wb = wbufs[i % 2][:, 0:width]
            p.wait("sync", "s", srel[i % 2])
            lv = p.dma("sync", lambda e, wb=wb, i=i, src_fn=src_fn: e.dma_start(
                out=wb, in_=src_fn(i)))
            p.wait("scalar", "ld", lv)
            p.wait("scalar", "v", vrel[i % 2])
            p.do("scalar", lambda e, wb=wb, width=width, tau_col=tau_col:
                 e.activation(out=sg1[:, 0:width], in_=wb, func=AF.Sign,
                              bias=negtau[:, tau_col:tau_col + 1]))
            sv = p.do("scalar", lambda e, wb=wb, width=width, tau_col=tau_col:
                      e.activation(out=sg2[:, 0:width], in_=wb, func=AF.Sign,
                                   bias=scalv[:, 2 + tau_col:3 + tau_col]), "s")
            srel[i % 2] = sv
            p.wait("vector", "s", sv)
            vv = p.do("vector", lambda e, width=width: e.tensor_tensor(
                out=tstage[:, 0:width], in0=sg1[:, 0:width], in1=sg2[:, 0:width],
                op=AL.add), "v")
            vrel[i % 2] = vv
            p.wait("sync", "v", vv)
            stv = p.dma("sync", lambda e, i=i, width=width, dst_fn=dst_fn:
                        e.dma_start(out=dst_fn(i), in_=tstage[:, 0:width]), "st")
        return stv

    tsyn_st = tbuild(64, lambda i: c.W_syn[i * 128:(i + 1) * 128, :],
                     lambda i: c.tsyn[i * 128:(i + 1) * 128, :], 2048, 0)
    tout_st = tbuild(16, lambda i: c.W_out[i * 128:(i + 1) * 128, :],
                     lambda i: c.tout[i * 128:(i + 1) * 128, :], NFF, 1)
    tb_s_done = max(srel)
    tb_v_done = max(vrel)

    TT = [[m.bufA[:, (par * 16 + dc) * 1024:(par * 16 + dc + 1) * 1024]
           for dc in range(16)] for par in range(2)]
    xnfg = [m.bufA[:, 32768 + dc * 1024: 32768 + (dc + 1) * 1024]
            for dc in range(16)]
    hT = [m.bufA[:, n * 512:(n + 1) * 512] for n in range(64)]
    ToutT = [m.bufA[:, 32768 + j * 1024: 32768 + (j + 1) * 1024] for j in range(2)]
    xnu = [m.bufA[:, 36864 + t * 2048: 36864 + (t + 1) * 2048] for t in range(4)]
    gbuf = m.bufA.bitcast(F32)[:, 22528:24576]
    zbuf = m.bufB[:, 0:8192]
    zsbuf = m.bufB[:, 8192:16384]
    hbuf = m.bufB[:, 8192:16384].bitcast(BF16)[:, 0:8192]
    sink = hbuf
    utok = m.bufC[:, 0:2048]
    gam_bc = m.bufC[:, 2048:4096]
    bet_bc = m.bufC[:, 4096:6144]
    bout_bc = m.bufC[:, 6144:8192]

    lvg = p.dma("sync", lambda e: e.dma_start(out=gam_bc, in_=c.cst128[1]))
    lvb = p.dma("sync", lambda e: e.dma_start(out=bet_bc, in_=c.cst128[2]))
    lvo = p.dma("sync", lambda e: e.dma_start(out=bout_bc, in_=c.cst128[0]))
    p.wait("vector", "ld", max(lvg, lvb, lvo))

    pool = PsumPool(p, m.psb, consumer_sem="v")
    z_ready = {}
    h_ready = {}
    b1_last_t = {"v": 0}
    zstg = [sm[:, 2048 + i * 512: 2048 + (i + 1) * 512] for i in range(2)]
    zstg_rel = [0, 0]
    guards = {"bsl": 0, "tt": [0, 0], "b2_vv": 0, "b2_hst": 0, "g_st": 0,
              "b3_t": 0}

    def load_xnfg(g):
        return [p.dma("sync", lambda e, dc=dc, g=g: e.dma_start(
            out=xnfg[dc],
            in_=c.xn_own[g * 1024:(g + 1) * 1024, dc * 128:(dc + 1) * 128],
            transpose=True)) for dc in range(16)]

    def b1_eighth(g, e8, xnv, tt_wait_extra=0):
        par = e8 % 2
        p.wait("sync", "v", guards["bsl"])
        bv = p.dma("sync", lambda e, e8=e8: e.dma_start(
            out=bsl, in_=c.bsyn128[:, e8 * 1024:(e8 + 1) * 1024]))
        p.wait("sync", "t", guards["tt"][par])
        p.wait("sync", "st", tsyn_st)
        ttv = [p.dma("sync", lambda e, dc=dc, e8=e8, par=par: e.dma_start(
            out=TT[par][dc],
            in_=c.tsyn[e8 * 1024:(e8 + 1) * 1024, dc * 128:(dc + 1) * 128],
            transpose=True)) for dc in range(16)]
        p.wait("tensor", "ld", max(max(ttv), max(xnv), tt_wait_extra))
        for tk in range(8):
            acq = [pool.acquire() for _ in range(2)]
            tv = [None, None]
            for dc in range(16):
                for j in range(2):
                    bi, bank = acq[j]
                    vv = p.do("tensor", lambda e, dc=dc, tk=tk, j=j, par=par,
                              bank=bank: e.matmul(
                        bank[:, :], xnfg[dc][:, tk * 128:(tk + 1) * 128],
                        TT[par][dc][:, j * 512:(j + 1) * 512],
                        start=(dc == 0), stop=(dc == 15)),
                        "t" if dc == 15 else None)
                    if vv is not None:
                        tv[j] = vv
            b1_last_t["v"] = max(tv)
            guards["tt"][par] = max(tv)
            for j in range(2):
                bi, bank = acq[j]
                p.wait("vector", "t", tv[j])
                if tk == 0 and j == 0:
                    p.wait("vector", "ld", bv)
                p.wait("vector", "st", zstg_rel[j])
                stg = zstg[j]
                col = e8 * 1024 + j * 512
                row = g * 1024 + tk * 128
                vv = p.do("vector", lambda e, bank=bank, j=j, stg=stg:
                          e.scalar_tensor_tensor(
                    out=stg, in0=bank[:, :], scalar=scalv[:, 0:1],
                    in1=bsl[:, j * 512:(j + 1) * 512], op0=AL.mult, op1=AL.add), "v")
                pool.released(bi, vv)
                guards["bsl"] = vv
                p.wait("sync", "v", vv)
                sv = p.dma("sync", lambda e, row=row, col=col, stg=stg: e.dma_start(
                    out=c.z_sp[row:row + 128, col:col + 512],
                    in_=stg), "st")
                zstg_rel[j] = sv
                z_ready[(g, tk)] = sv

    def gapop(e):
        return e.tensor_copy(out=ST["gap"], in_=ST["mu"])

    def vts(out, in0, s1, s2=None, op0=AL.mult, op1=None, accum=None):
        def f(e):
            kw = {}
            if accum is not None:
                kw["accum_out"] = accum
            if op1 is not None:
                return e.tensor_scalar(out=out, in0=in0, scalar1=s1, scalar2=s2,
                                       op0=op0, op1=op1, **kw)
            return e.tensor_scalar(out=out, in0=in0, scalar1=s1, scalar2=None,
                                   op0=op0, **kw)
        return f

    def b2_tile(g, tk):
        row = g * 1024 + tk * 128
        p.wait("sync", "st", z_ready[(g, tk)])
        p.wait("sync", "v", max(guards["b2_vv"], tb_v_done))
        p.wait("sync", "s", tb_s_done)
        zlv = p.dma("sync", lambda e, row=row: e.dma_start(
            out=zbuf, in_=c.z_sp[row:row + 128, :]))
        p.wait("vector", "ld", zlv)
        p.wait("vector", "st", guards["b2_hst"])
        p.wait("scalar", "ld", zlv)
        p.wait("scalar", "st", guards["b2_hst"])
        p.do("vector", lambda e: e.tensor_scalar(
            out=zsbuf, in0=zbuf, scalar1=0.0, scalar2=None, op0=AL.add,
            op1=AL.add, accum_out=ST["sz"]))
        sv = p.do("scalar", lambda e: e.activation(
            out=sink, in_=zbuf, func=AF.Square, accum_out=ST["sz2"]), "s")
        p.wait("vector", "s", sv)
        p.do("vector", vts(ST["mu"], ST["sz"], 1.0 / NFF))
        p.do("vector", vts(ST["t1"], ST["sz2"], 1.0 / NFF))
        p.do("vector", lambda e: e.scalar_tensor_tensor(
            out=ST["t2"], in0=ST["mu"], scalar=-1.0, in1=ST["mu"],
            op0=AL.mult, op1=AL.mult))
        p.do("vector", gapop)
        p.do("vector", lambda e: e.tensor_tensor(
            out=ST["var"], in0=ST["t1"], in1=ST["t2"], op=AL.add))
        p.do("vector", gapop)
        p.do("vector", gapop)
        p.do("vector", vts(ST["sb"], ST["var"], 1.0, 1.6016, op0=AL.add,
                           op1=AL.mult))
        p.do("vector", gapop)
        p.do("vector", gapop)
        p.do("vector", vts(ST["sb"], ST["sb"], 1e-6, op0=AL.add))
        p.do("vector", gapop)
        p.do("vector", gapop)
        p.do("vector", lambda e: e.tensor_tensor(
            out=ST["hi"], in0=ST["mu"], in1=ST["sb"], op=AL.add))
        p.do("vector", lambda e: e.tensor_tensor(
            out=ST["lo"], in0=ST["mu"], in1=ST["sb"], op=AL.subtract))
        vv = p.do("vector", vts(ST["negv"], ST["hi"], -1.0), "v")
        for it in range(J0 + 1):
            p.wait("scalar", "v", vv)
            sv = p.do("scalar", lambda e: e.activation(
                out=sink, in_=zbuf, func=AF.Sign, bias=ST["negv"],
                accum_out=ST["cnt"]), "s")
            p.wait("vector", "s", sv)
            if it == 0:
                p.do("vector", vts(ST["chi"], ST["cnt"], float(NFF), 0.5,
                                   op0=AL.add, op1=AL.mult))
                p.do("vector", gapop)
                vv = p.do("vector", lambda e: e.scalar_tensor_tensor(
                    out=ST["negv"], in0=ST["lo"], scalar=ST["hi"],
                    in1=konst[:, 0:1], op0=AL.add, op1=AL.mult), "v")
                continue
            p.do("vector", vts(ST["cg"], ST["cnt"], float(NFF), 0.5,
                               op0=AL.add, op1=AL.mult))
            p.do("vector", vts(ST["v"], ST["negv"], -1.0))
            p.do("vector", vts(ST["pred"], ST["cg"], float(KTOP) - 0.25,
                               op0=AL.is_ge))
            p.do("vector", vts(ST["t2"], ST["cg"], float(KTOP) - 0.25,
                               op0=AL.is_lt))
            p.do("vector", gapop)
            pm = ST["pred"].bitcast(mybir.dt.uint32)
            nm = ST["t2"].bitcast(mybir.dt.uint32)
            p.do("vector", lambda e, pm=pm: e.copy_predicated(
                out=ST["lo"], mask=pm, data=ST["v"]))
            p.do("vector", lambda e, nm=nm: e.copy_predicated(
                out=ST["hi"], mask=nm, data=ST["v"]))
            p.do("vector", lambda e, nm=nm: e.copy_predicated(
                out=ST["chi"], mask=nm, data=ST["cg"]))
            vv = p.do("vector", lambda e: e.scalar_tensor_tensor(
                out=ST["negv"], in0=ST["lo"], scalar=ST["hi"], in1=konst[:, 0:1],
                op0=AL.add, op1=AL.mult), "v")
        p.do("vector", vts(ST["neglo"], ST["lo"], -1.0))
        vv = p.do("vector", gapop, "v")
        p.wait("scalar", "v", vv)
        sv = p.do("scalar", lambda e: e.activation(
            out=zsbuf, in_=zbuf, func=AF.Identity, bias=ST["neglo"]), "s")
        p.wait("vector", "s", sv)
        p.do("vector", lambda e: e.scalar_tensor_tensor(
            out=zsbuf, in0=zbuf, scalar=ST["hi"], in1=zsbuf,
            op0=AL.is_le, op1=AL.mult))
        p.do("vector", lambda e: e.max(out=top16[:, 0:8], in_=zsbuf))
        p.do("vector", gapop)
        p.do("vector", gapop)
        p.do("vector", lambda e: e.match_replace(
            out=zsbuf, in_to_replace=top16[:, 0:8], in_values=zsbuf,
            imm_value=-1.0e30))
        p.do("vector", lambda e: e.max(out=top16[:, 8:16], in_=zsbuf))
        p.do("vector", vts(ST["t1"], ST["chi"], -1.0, float(KTOP),
                           op0=AL.mult, op1=AL.add))
        p.do("vector", gapop)
        p.do("vector", gapop)
        p.do("vector", vts(oh16, iota16, ST["t1"], op0=AL.is_equal))
        p.do("vector", gapop)
        p.do("vector", gapop)
        p.do("vector", lambda e: e.tensor_tensor(
            out=oh16, in0=oh16, in1=top16, op=AL.mult))
        p.do("vector", gapop)
        p.do("vector", gapop)
        p.do("vector", vts(oh16, oh16, 0.0, op0=AL.add, op1=AL.add,
                           accum=ST["vstar"]))
        p.do("vector", gapop)
        p.do("vector", gapop)
        p.do("vector", lambda e: e.scalar_tensor_tensor(
            out=ST["vstar"], in0=ST["vstar"], scalar=ST["lo"], in1=ST["lo"],
            op0=AL.add, op1=AL.max))
        p.do("vector", gapop)
        p.do("vector", gapop)
        p.do("vector", vts(ST["vstar"], ST["vstar"], 0.0, op0=AL.max))
        p.do("vector", gapop)
        p.do("vector", gapop)
        vv = p.do("vector", lambda e: e.scalar_tensor_tensor(
            out=hbuf, in0=zbuf, scalar=ST["vstar"], in1=zbuf,
            op0=AL.is_gt, op1=AL.mult), "v")
        p.wait("sync", "v", vv)
        guards["b2_vv"] = vv
        if (g, tk) == (1, 7):
            p.dma("sync", lambda e: e.dma_start(
                out=c.dbg_small[:, :], in_=sm[:, 0:1024]), "st")
        hv = p.dma("sync", lambda e, row=row: e.dma_start(
            out=c.h_sp[row:row + 128, :], in_=hbuf), "st")
        h_ready[(g, tk)] = hv
        guards["b2_hst"] = hv

    def b3_sub(gsub):
        rows0 = gsub * 512
        g = gsub // 2
        tks = [gsub % 2 * 4 + i for i in range(4)]
        p.wait("sync", "st", max(h_ready[(g, tk)] for tk in tks))
        p.wait("sync", "t", max(b1_last_t["v"], guards["b3_t"]))
        hv = [p.dma("sync", lambda e, n=n, rows0=rows0: e.dma_start(
            out=hT[n], in_=c.h_sp[rows0:rows0 + 512, n * 128:(n + 1) * 128],
            transpose=True)) for n in range(64)]
        xv = [p.dma("sync", lambda e, t=t, rows0=rows0: e.dma_start(
            out=xnu[t], in_=c.xn_own[rows0 + t * 128:rows0 + (t + 1) * 128, :]))
            for t in range(4)]
        to_guard = [0, 0]
        for half in range(2):
            for i in range(8):
                p.wait("tensor", "v", pool.release[i])
            tv = [None] * 8
            for n in range(64):
                p.wait("sync", "t", to_guard[n % 2])
                tov = p.dma("sync", lambda e, n=n, half=half: e.dma_start(
                    out=ToutT[n % 2],
                    in_=c.tout[half * 1024:(half + 1) * 1024,
                               n * 128:(n + 1) * 128],
                    transpose=True))
                p.wait("tensor", "ld", tov)
                if n == 0 and half == 0:
                    p.wait("tensor", "ld", max(max(hv), max(xv)))
                    p.wait("tensor", "st", tout_st)
                for ts in range(4):
                    for dq in range(2):
                        bank = m.psb[ts * 2 + dq]
                        vv = p.do("tensor", lambda e, n=n, ts=ts, dq=dq,
                                  bank=bank, half=half: e.matmul(
                            bank[:, :], hT[n][:, ts * 128:(ts + 1) * 128],
                            ToutT[n % 2][:, dq * 512:(dq + 1) * 512],
                            start=(n == 0), stop=(n == 63)),
                            "t" if (n == 63 or (ts == 3 and dq == 1)) else None)
                        if vv is not None and n == 63:
                            tv[ts * 2 + dq] = vv
                        if vv is not None and ts == 3 and dq == 1:
                            to_guard[n % 2] = vv
                            guards["b3_t"] = vv
            for ts in range(4):
                for dq in range(2):
                    bi = ts * 2 + dq
                    bank = m.psb[bi]
                    cols = half * 1024 + dq * 512
                    p.wait("vector", "t", tv[bi])
                    p.do("vector", lambda e, bank=bank, ts=ts, cols=cols:
                         e.scalar_tensor_tensor(
                             out=utok[:, cols:cols + 512], in0=bank[:, :],
                             scalar=scalv[:, 1:2],
                             in1=bout_bc[:, cols:cols + 512],
                             op0=AL.mult, op1=AL.add))
                    p.do("vector", lambda e: e.tensor_copy(
                        out=ST["gap"], in_=ST["mu"]))
                    vv = p.do("vector", lambda e, ts=ts, cols=cols:
                              e.tensor_tensor(
                                  out=ubuf[ts][:, cols:cols + 512],
                                  in0=utok[:, cols:cols + 512],
                                  in1=xnu[ts][:, cols:cols + 512], op=AL.add), "v")
                    pool.release[bi] = vv
        for ts in range(4):
            ln2_silu(gsub, ts)

    ubuf = xnu

    def ln2_silu(gsub, ts):
        u = ubuf[ts]
        rows0 = gsub * 512 + ts * 128
        p.do("vector", lambda e: e.tensor_scalar(
            out=utok, in0=u, scalar1=0.0, scalar2=None, op0=AL.add,
            op1=AL.add, accum_out=ST["sz"]))
        p.do("vector", lambda e: e.scalar_tensor_tensor(
            out=utok, in0=u, scalar=0.0, in1=u, op0=AL.add,
            op1=AL.mult, accum_out=ST["sz2"]))
        p.do("vector", vts(ST["mu"], ST["sz"], 1.0 / D))
        p.do("vector", vts(ST["t1"], ST["sz2"], 1.0 / D))
        p.do("vector", lambda e: e.scalar_tensor_tensor(
            out=ST["t2"], in0=ST["mu"], scalar=-1.0, in1=ST["mu"],
            op0=AL.mult, op1=AL.mult))
        p.do("vector", gapop)
        p.do("vector", lambda e: e.tensor_tensor(
            out=ST["var"], in0=ST["t1"], in1=ST["t2"], op=AL.add))
        p.do("vector", gapop)
        p.do("vector", gapop)
        vv = p.do("vector", vts(ST["t1"], ST["var"], LN_EPS, op0=AL.add), "v")
        p.wait("scalar", "v", vv)
        p.do("scalar", lambda e: e.activation(out=ST["t2"], in_=ST["t1"],
                                              func=AF.Ln))
        p.do("scalar", lambda e: e.activation(out=ST["gap"], in_=ST["t1"],
                                              func=AF.Ln))
        sv = p.do("scalar", lambda e: e.activation(
            out=ST["v"], in_=ST["t2"], func=AF.Exp, scale=-0.5), "s")
        p.wait("vector", "s", sv)
        p.do("vector", lambda e: e.scalar_tensor_tensor(
            out=ST["t2"], in0=ST["v"], scalar=1.0, in1=ST["v"],
            op0=AL.mult, op1=AL.mult))
        p.do("vector", gapop)
        p.do("vector", gapop)
        p.do("vector", lambda e: e.scalar_tensor_tensor(
            out=ST["t2"], in0=ST["t2"], scalar=-0.5, in1=ST["t1"],
            op0=AL.mult, op1=AL.mult))
        p.do("vector", gapop)
        p.do("vector", gapop)
        p.do("vector", vts(ST["t2"], ST["t2"], 1.5, op0=AL.add))
        p.do("vector", gapop)
        p.do("vector", gapop)
        p.do("vector", lambda e: e.tensor_tensor(
            out=ST["v"], in0=ST["v"], in1=ST["t2"], op=AL.mult))
        p.do("vector", gapop)
        p.do("vector", gapop)
        p.wait("vector", "st", guards["g_st"])
        p.do("vector", lambda e: e.tensor_scalar(
            out=gbuf, in0=u, scalar1=ST["mu"], scalar2=ST["v"],
            op0=AL.subtract, op1=AL.mult))
        p.do("vector", lambda e: e.tensor_tensor(
            out=gbuf, in0=gbuf, in1=gam_bc, op=AL.mult))
        vv = p.do("vector", lambda e: e.tensor_tensor(
            out=gbuf, in0=gbuf, in1=bet_bc, op=AL.add), "v")
        p.wait("scalar", "v", vv)
        sv = p.do("scalar", lambda e: e.activation(
            out=gbuf, in_=gbuf, func=AF.Silu), "s")
        p.wait("sync", "s", sv)
        guards["g_st"] = p.dma("sync", lambda e, rows0=rows0: e.dma_start(
            out=c.g_sp[rows0:rows0 + 128, :], in_=gbuf), "st")

    xnv0 = load_xnfg(0)
    for e8 in range(8):
        b1_eighth(0, e8, xnv0, tt_wait_extra=0)
    p.wait("sync", "t", b1_last_t["v"])
    xnv1 = load_xnfg(1)
    b2q = [(0, tk) for tk in range(8)] + [(1, tk) for tk in range(8)]
    b2i = 0
    for e8 in range(8):
        b1_eighth(1, e8, xnv1)
        if e8 >= 2 and b2i < 6:
            b2_tile(*b2q[b2i]); b2i += 1
    while b2i < 8:
        b2_tile(*b2q[b2i]); b2i += 1
    b3_sub(0)
    b2_tile(*b2q[b2i]); b2i += 1
    b2_tile(*b2q[b2i]); b2i += 1
    b3_sub(1)
    b2_tile(*b2q[b2i]); b2i += 1
    b2_tile(*b2q[b2i]); b2i += 1
    b3_sub(2)
    while b2i < 16:
        b2_tile(*b2q[b2i]); b2i += 1
    b3_sub(3)
    p.wait("sync", "st", p.cur("st"))


def phase3(p, c, m):
    sm = m.small
    slab = m.bufA[:, 0:32768]
    kbuf = [m.bufA[:, 32768 + i * 4096: 32768 + (i + 1) * 4096] for i in range(2)]
    vbuf = [m.bufA[:, 40960 + i * 4096: 40960 + (i + 1) * 4096] for i in range(2)]
    cmask = [m.bufB[:, i * 512:(i + 1) * 512] for i in range(9)]
    bufBb = m.bufB.bitcast(BF16)
    qbuf = [bufBb[:, 9216 + i * 2048: 9216 + (i + 1) * 2048] for i in range(2)]
    pt = [bufBb[:, 13312 + i * 512: 13312 + (i + 1) * 512] for i in range(4)]
    rln = m.bufB[0:1, 7680:8192]
    rrow = m.bufB[0:1, 8192:8704]
    ones_s = m.bufB[0:1, 8704:8832]
    onesc_bf = bufBb[:, 17664:17665]
    wot = [bufBb[:, i * 2048:(i + 1) * 2048] for i in range(16)]
    gsl = m.bufC[:, 0:2048]
    xsl = m.bufC[:, 2048:4096]
    osl = m.bufC[:, 4096:6144]
    rbc = m.bufC[:, 6144:6656]

    p.wait("sync", "v", p.cur("v"))
    p.wait("sync", "s", p.cur("s"))
    p.wait("sync", "t", p.cur("t"))
    p.wait("tensor", "v", p.cur("v"))
    p.wait("tensor", "s", p.cur("s"))
    mlv = [p.dma("sync", lambda e, i=i: e.dma_start(out=cmask[i], in_=c.cmask[i]))
           for i in range(9)]
    lvo = p.dma("sync", lambda e: e.dma_start(out=ones_s, in_=c.ones_row_f))
    lvc = p.dma("sync", lambda e: e.dma_start(out=onesc_bf, in_=c.ones_col_bf))
    p.wait("vector", "ld", max(mlv))

    kq_rel = [0, 0]
    sb_rel = [0, 0, 0, 0]
    pt_rel = [0, 0, 0, 0]
    att_rel = [0, 0]
    r6_rel = [0, 0, 0, 0]
    r7_rel = [0]
    uidx = [0]

    def head(h):
        par2 = h % 2
        p.wait("sync", "t", kq_rel[par2])
        kv = p.dma("sync", lambda e, h=h: e.dma_start(
            out=kbuf[par2], in_=c.kf[h * 128:(h + 1) * 128, :]))
        vv0 = p.dma("sync", lambda e, h=h: e.dma_start(
            out=vbuf[par2].rearrange("p (kc d) -> p kc d", d=128),
            in_=c.vt[h].rearrange("(kc p) d -> p kc d", p=128)))
        qv = p.dma("sync", lambda e, h=h: e.dma_start(
            out=qbuf[par2], in_=c.qf[h * 128:(h + 1) * 128, :]))
        ldv = max(kv, vv0, qv, lvc)
        first = True
        for qc in range(4):
            lim = min(8 * qc + 8, 31)
            abank = m.psb[2 + (qc % 2)]
            for kc in range(lim + 1):
                u = uidx[0]; uidx[0] += 1
                sb = u % 4
                sbank = m.psb[[0, 1, 4, 5][sb]]
                p.wait("tensor", "s", sb_rel[sb])
                if first:
                    p.wait("tensor", "ld", ldv)
                    first = False
                tv = p.do("tensor", lambda e, kc=kc, qc=qc, sbank=sbank:
                          e.matmul(sbank[:, :],
                                   kbuf[par2][:, kc * 128:(kc + 1) * 128],
                                   qbuf[par2][:, qc * 512:(qc + 1) * 512],
                                   start=True, stop=True), "t")
                r = kc - 8 * qc
                if 0 <= r <= 8:
                    p.wait("vector", "t", tv)
                    mv = p.do("vector", lambda e, r=r, sbank=sbank:
                              e.tensor_tensor(out=sbank[:, :], in0=sbank[:, :],
                                              in1=cmask[r], op=AL.add), "v")
                    p.wait("scalar", "v", mv)
                else:
                    p.wait("scalar", "t", tv)
                myp = u % 4
                p.wait("scalar", "t", pt_rel[myp])
                sv = p.do("scalar", lambda e, sbank=sbank, myp=myp: e.activation(
                    out=pt[myp], in_=sbank[:, :], func=AF.Exp,
                    scale=INV_SQRT_HD), "s")
                sb_rel[sb] = sv
                p.wait("tensor", "s", sv)
                if kc == 0:
                    p.wait("tensor", "v", att_rel[qc % 2])
                    p.wait("tensor", "s", max(r6_rel))
                p.do("tensor", lambda e, kc=kc, myp=myp, abank=abank, lim=lim:
                     e.matmul(abank[:, :],
                              vbuf[par2][:, kc * 128:(kc + 1) * 128], pt[myp],
                              start=(kc == 0), stop=(kc == lim)))
                tv2 = p.do("tensor", lambda e, myp=myp, kc=kc, lim=lim:
                           e.matmul(m.psb[6][0:1, :], onesc_bf, pt[myp],
                                    start=(kc == 0), stop=(kc == lim)), "t")
                pt_rel[myp] = tv2
            p.wait("scalar", "t", tv2)
            p.do("scalar", lambda e: e.activation(
                out=rln, in_=m.psb[6][0:1, :], func=AF.Ln))
            p.do("scalar", lambda e: e.activation(
                out=rrow, in_=m.psb[6][0:1, :], func=AF.Ln))
            sv = p.do("scalar", lambda e: e.activation(
                out=rrow, in_=rln, func=AF.Exp, scale=-1.0), "s")
            r6_rel[qc] = sv
            p.wait("tensor", "s", sv)
            p.wait("tensor", "ld", lvo)
            p.wait("tensor", "v", r7_rel[0])
            tv3 = p.do("tensor", lambda e: e.matmul(
                m.psb[7][:, :], ones_s, rrow, start=True, stop=True), "t")
            p.wait("vector", "t", tv3)
            if h == 0 and qc == 0 and DBG_ATT:
                p.do("vector", lambda e: e.tensor_copy(
                    out=m.bufC[:, 6656:7168], in_=abank[:, :]))
                p.do("vector", lambda e: e.tensor_copy(
                    out=m.bufC[0:1, 7168:7680], in_=rln))
                dv = p.do("vector", lambda e: e.tensor_copy(
                    out=m.bufC[0:1, 7680:8192], in_=rrow), "v")
                p.wait("sync", "v", dv)
                p.dma("sync", lambda e: e.dma_start(
                    out=c.dbg_att[:, 0:512], in_=m.bufC[:, 6656:7168]), "st")
                p.dma("sync", lambda e: e.dma_start(
                    out=c.dbg_att[0:1, 512:1024], in_=m.bufC[0:1, 7168:7680]), "st")
                p.dma("sync", lambda e: e.dma_start(
                    out=c.dbg_att[0:1, 1024:1536], in_=m.bufC[0:1, 7680:8192]), "st")
            p.do("vector", lambda e: e.tensor_copy(out=rbc, in_=m.psb[7][:, :]))
            p.do("vector", lambda e: e.tensor_copy(out=sm[:, 0:1], in_=sm[:, 2:3]))
            p.do("vector", lambda e: e.tensor_copy(out=sm[:, 1:2], in_=sm[:, 2:3]))
            vv2 = p.do("vector", lambda e, h=h, qc=qc, abank=abank: e.tensor_tensor(
                out=slab[:, h * 2048 + qc * 512: h * 2048 + (qc + 1) * 512],
                in0=abank[:, :], in1=rbc, op=AL.mult), "v")
            att_rel[qc % 2] = vv2
            r7_rel[0] = vv2
        kq_rel[par2] = p.cur("t")

    for h in range(P3_HEADS):
        head(h)

    if not P3_WO:
        for hh in range(16):
            vvx = p.do("vector", lambda e, hh=hh: e.tensor_copy(
                out=osl, in_=slab[:, hh * 2048:(hh + 1) * 2048]), "v")
            p.wait("sync", "v", vvx)
            sx = p.dma("sync", lambda e, hh=hh: e.dma_start(
                out=c.y[hh * 128:(hh + 1) * 128, :], in_=osl), "st")
            p.wait("vector", "st", sx)
        p.wait("sync", "st", p.cur("st"))
        p.wait("scalar", "st", p.cur("st"))
        p.wait("vector", "st", p.cur("st"))
        p.wait("tensor", "st", p.cur("st"))
        p.wait("gpsimd", "st", p.cur("st"))
        return

    p.wait("sync", "t", p.cur("t"))
    p.wait("sync", "s", p.cur("s"))
    p.wait("sync", "v", p.cur("v"))
    wlv = [p.dma("sync", lambda e, fc=fc: e.dma_start(
        out=wot[fc], in_=c.wo_bf[:, fc * 128:(fc + 1) * 128], transpose=True))
        for fc in range(16)]
    wo_rel = [0, 0, 0, 0]
    for qt in range(16):
        glv = p.dma("sync", lambda e, qt=qt: e.dma_start(
            out=gsl, in_=c.g_sp[qt * 128:(qt + 1) * 128, :]))
        xlv = p.dma("sync", lambda e, qt=qt: e.dma_start(
            out=xsl, in_=c.x_own[qt * 128:(qt + 1) * 128, :]))
        tvs = [None] * 4
        if qt == 0:
            p.wait("tensor", "ld", max(wlv))
        for fc in range(16):
            for oc in range(4):
                bank = m.psb[oc]
                if qt == 0 and fc == 0 and oc == 0:
                    p.wait("tensor", "v", p.cur("v"))
                    p.wait("tensor", "s", p.cur("s"))
                if fc == 0:
                    p.wait("tensor", "v", wo_rel[oc])
                vv = p.do("tensor", lambda e, fc=fc, oc=oc, qt=qt, bank=bank:
                          e.matmul(
                    bank[:, :],
                    slab[:, fc * 2048 + qt * 128: fc * 2048 + (qt + 1) * 128],
                    wot[fc][:, oc * 512:(oc + 1) * 512],
                    start=(fc == 0), stop=(fc == 15)),
                    "t" if fc == 15 else None)
                if vv is not None:
                    tvs[oc] = vv
        p.wait("vector", "ld", max(glv, xlv))
        for oc in range(4):
            bank = m.psb[oc]
            p.wait("vector", "t", tvs[oc])
            wo_rel[oc] = p.do("vector", lambda e, oc=oc, bank=bank: e.tensor_tensor(
                out=osl[:, oc * 512:(oc + 1) * 512], in0=bank[:, :],
                in1=gsl[:, oc * 512:(oc + 1) * 512], op=AL.mult), "v")
        p.do("vector", lambda e: e.tensor_copy(out=sm[:, 0:1], in_=sm[:, 2:3]))
        p.do("vector", lambda e: e.tensor_copy(out=sm[:, 1:2], in_=sm[:, 2:3]))
        vv = None
        for oc in range(4):
            vv = p.do("vector", lambda e, oc=oc: e.tensor_tensor(
                out=osl[:, oc * 512:(oc + 1) * 512],
                in0=osl[:, oc * 512:(oc + 1) * 512],
                in1=xsl[:, oc * 512:(oc + 1) * 512], op=AL.add), "v")
        p.wait("sync", "v", vv)
        p.dma("sync", lambda e, qt=qt: e.dma_start(
            out=c.y[qt * 128:(qt + 1) * 128, :], in_=osl), "st")
        p.wait("sync", "st", p.cur("st"))
    p.wait("sync", "st", p.cur("st"))
    p.wait("scalar", "st", p.cur("st"))
    p.wait("vector", "st", p.cur("st"))
    p.wait("tensor", "st", p.cur("st"))
    p.wait("gpsimd", "st", p.cur("st"))


def shard_inputs(x, w_qkv, w_o, W_syn, b_syn, W_out, b_out, gamma, beta):
    in_maps = []
    own_rows = {}
    for ci in range(NCORES):
        b, par = divmod(ci, 2)
        rows = np.concatenate([np.arange((2 * i + par) * 128, (2 * i + par + 1) * 128)
                               for i in range(NOWN)])
        own_rows[ci] = rows
        in_maps.append({
            "x_seq": np.ascontiguousarray(x[b]),
            "x_own": np.ascontiguousarray(x[b][rows]),
            "cmask": host_masks(par),
            "w_qkv": np.ascontiguousarray(w_qkv, dtype=np.float32),
            "w_o": np.ascontiguousarray(w_o, dtype=np.float32),
            "W_syn": np.ascontiguousarray(W_syn, dtype=np.float32),
            "b_syn": np.ascontiguousarray(b_syn, dtype=np.float32),
            "W_out": np.ascontiguousarray(W_out, dtype=np.float32),
            "b_out": np.ascontiguousarray(b_out, dtype=np.float32),
            "gamma": np.ascontiguousarray(gamma, dtype=np.float32),
            "beta": np.ascontiguousarray(beta, dtype=np.float32),
        })
    return in_maps, own_rows


ALL_PHASES = ("p0", "p1", "p2", "p3")


def kernel(L0_attn_hidden_states, L0_attn_w_qkv, L0_attn_w_o, L0_ffn_W_syn,
           L0_ffn_b_syn, L0_ffn_W_out, L0_ffn_b_out, L0_ffn_gamma, L0_ffn_beta):
    x = np.ascontiguousarray(L0_attn_hidden_states, dtype=np.float32)
    nc = build_program(phases=ALL_PHASES)
    in_maps, own_rows = shard_inputs(
        x, L0_attn_w_qkv, L0_attn_w_o, L0_ffn_W_syn, L0_ffn_b_syn,
        L0_ffn_W_out, L0_ffn_b_out, L0_ffn_gamma, L0_ffn_beta)
    res = run_bass_kernel_spmd(nc, in_maps, core_ids=list(range(NCORES)))
    out = np.empty((B, S, D), dtype=np.float32)
    for ci in range(NCORES):
        out[ci // 2][own_rows[ci]] = res.results[ci]["y"]
    return out


# version 02
# speedup vs baseline: 1.0056x; 1.0056x over previous
import math
import numpy as np
import ml_dtypes

import concourse.bass as bass
import concourse.mybir as mybir
from concourse.bass_utils import run_bass_kernel_spmd

F32 = mybir.dt.float32
BF16 = mybir.dt.bfloat16
AF = mybir.ActivationFunctionType
AL = mybir.AluOpType

B, S, D = 4, 4096, 2048
NH, HD = 16, 128
NFF = 8192
KTOP = 819
NCORES = 8
OWN = S // 2
NBLK = S // 128
NOWN = OWN // 128
INV_SQRT_HD = 1.0 / math.sqrt(128.0)
LN_EPS = 1e-5
NEG = -1.0e30
J0 = 13
P3_HEADS = NH
P3_WO = True
DBG_ATT = False

ENGINES = ("sync", "scalar", "vector", "tensor", "gpsimd")


class Plan:

    def __init__(self, nc, sem_names):
        self.nc = nc
        self.ops = {e: [] for e in ENGINES}
        self.sems = {}
        self.val = {}
        self.ctxs = []
        for name in sem_names:
            ctx = nc.semaphore(name)
            self.sems[name] = ctx.__enter__()
            self.ctxs.append(ctx)
            self.val[name] = 0

    def do(self, eng, fn, inc=None, n=1):
        if inc is None:
            self.ops[eng].append(("op", fn, None, 0))
            return None
        self.val[inc] += n
        self.ops[eng].append(("op", fn, self.sems[inc], n))
        return self.val[inc]

    def dma(self, eng, fn, inc="ld"):
        return self.do(eng, fn, inc, 16)

    def wait(self, eng, sem, val):
        if val is None or val <= 0:
            return
        self.ops[eng].append(("wait", None, self.sems[sem], val))

    def cur(self, sem):
        return self.val[sem]

    def replay(self, block):
        def runner(ops):
            def _(eng):
                for kind, fn, sem, n in ops:
                    if kind == "wait":
                        eng.wait_ge(sem, n)
                    else:
                        instr = fn(eng)
                        if sem is not None:
                            instr.then_inc(sem, n)
            return _

        block.sync(runner(self.ops["sync"]))
        block.scalar(runner(self.ops["scalar"]))
        block.vector(runner(self.ops["vector"]))
        block.tensor(runner(self.ops["tensor"]))
        block.gpsimd(runner(self.ops["gpsimd"]))


class PsumPool:

    def __init__(self, plan, banks, consumer_sem="v"):
        self.p = plan
        self.banks = list(banks)
        self.sem = consumer_sem
        self.release = [0] * len(self.banks)
        self.idx = 0

    def acquire(self):
        i = self.idx % len(self.banks)
        self.idx += 1
        self.p.wait("tensor", self.sem, self.release[i])
        return i, self.banks[i]

    def released(self, i, val):
        self.release[i] = val


def host_masks(par):
    masks = np.zeros((9, 128, 512), dtype=np.float32)
    for r in range(9):
        for s4 in range(4):
            off = (r - 2 * s4 - par) * 128
            for rr in range(128):
                lim = off + rr
                if lim <= 0:
                    continue
                masks[r, rr, s4 * 128: s4 * 128 + min(lim, 128)] = NEG
    return masks


class Ctx:
    pass


def declare(nc, debug_taps=()):
    c = Ctx()
    dt_ = nc.dram_tensor
    c.x_seq = dt_("x_seq", [S, D], F32, kind="ExternalInput").ap()
    c.x_own = dt_("x_own", [OWN, D], F32, kind="ExternalInput").ap()
    c.cmask = dt_("cmask", [9, 128, 512], F32, kind="ExternalInput").ap()
    c.w_qkv = dt_("w_qkv", [3 * D, D], F32, kind="ExternalInput").ap()
    c.w_o = dt_("w_o", [D, D], F32, kind="ExternalInput").ap()
    c.W_syn = dt_("W_syn", [NFF, D], F32, kind="ExternalInput").ap()
    c.b_syn = dt_("b_syn", [NFF], F32, kind="ExternalInput").ap()
    c.W_out = dt_("W_out", [D, NFF], F32, kind="ExternalInput").ap()
    c.b_out = dt_("b_out", [D], F32, kind="ExternalInput").ap()
    c.gamma = dt_("gamma", [D], F32, kind="ExternalInput").ap()
    c.beta = dt_("beta", [D], F32, kind="ExternalInput").ap()

    c.y = dt_("y", [OWN, D], F32, kind="ExternalOutput").ap()

    def scratch(name, shape, dtype):
        if name in debug_taps:
            return dt_(name, shape, dtype, kind="ExternalOutput").ap()
        return dt_(name, shape, dtype).ap()

    c.xn_tok = scratch("xn_tok", [S, D], BF16)
    c.xn_own = scratch("xn_own", [OWN, D], BF16)
    c.qf = scratch("qf", [D, OWN], BF16)
    c.kf = scratch("kf", [D, S], BF16)
    c.vt = scratch("vt", [NH, S, HD], BF16)
    c.wqkv_bf = scratch("wqkv_bf", [3 * D, D], BF16)
    c.wo_bf = scratch("wo_bf", [D, D], BF16)
    c.tsyn = scratch("tsyn", [NFF, D], BF16)
    c.tout = scratch("tout", [D, NFF], BF16)
    c.bsyn128 = scratch("bsyn128", [128, NFF], F32)
    c.cst128 = scratch("cst128", [3, 128, D], F32)
    c.scal128 = scratch("scal128", [128, 4], F32)
    c.dbg_small = scratch("dbg_small", [128, 1024], F32)
    c.dbg_att = scratch("dbg_att", [128, 2048], F32)
    c.z_sp = scratch("z_sp", [OWN, NFF], F32)
    c.h_sp = scratch("h_sp", [OWN, NFF], BF16)
    c.g_sp = scratch("g_sp", [OWN, D], F32)

    c.ones_row_f = nc.inline_tensor(np.ones((1, 128), dtype=np.float32),
                                    name="ones_row_f").ap()
    c.ones_col_f = nc.inline_tensor(np.ones((128, 1), dtype=np.float32),
                                    name="ones_col_f").ap()
    c.ones_col_bf = nc.inline_tensor(np.ones((128, 1), dtype=ml_dtypes.bfloat16),
                                     name="ones_col_bf").ap()
    iota16 = np.broadcast_to(np.arange(16, dtype=np.float32), (128, 16)).copy()
    c.iota16 = nc.inline_tensor(iota16, name="iota16").ap()
    konst = np.broadcast_to(np.array([-0.5, 1.0, 0.0, 0.0], dtype=np.float32),
                            (128, 4)).copy()
    c.konst = nc.inline_tensor(konst, name="konst").ap()
    return c



def phase0(p, c, m):
    xt = [m.bufB[:, 0:2048], m.bufB[:, 2048:4096]]
    junk = m.bufA[:, 0:2048]
    xnt = [m.bufA[:, 2048:4096], m.bufA[:, 4096:6144]]
    sm = m.small
    st_sums, st_sums2, st_m, st_t1, st_t2, st_r, st_d = (
        sm[:, 0:1], sm[:, 1:2], sm[:, 2:3], sm[:, 3:4], sm[:, 4:5], sm[:, 5:6],
        sm[:, 6:7])

    state = {"i": 0, "st": [0, 0]}

    def ln1_tile(src_rows, dst):
        i = state["i"]; state["i"] += 1
        xb, xo = xt[i % 2], xnt[i % 2]
        lv = p.dma("sync", lambda e, xb=xb, src=src_rows: e.dma_start(out=xb, in_=src))
        p.wait("vector", "ld", lv)
        p.wait("vector", "st", state["st"][i % 2])
        p.do("vector", lambda e, xb=xb: e.tensor_scalar(
            out=junk, in0=xb, scalar1=0.0, scalar2=None, op0=AL.add, op1=AL.add,
            accum_out=st_sums))
        p.do("vector", lambda e, xb=xb: e.scalar_tensor_tensor(
            out=m.bufA[:, 6144:8192], in0=xb, scalar=0.0, in1=xb,
            op0=AL.add, op1=AL.mult, accum_out=st_sums2))
        p.do("vector", lambda e: e.tensor_scalar(
            out=st_m, in0=st_sums, scalar1=1.0 / D, scalar2=None, op0=AL.mult))
        p.do("vector", lambda e: e.tensor_scalar(
            out=st_t1, in0=st_sums2, scalar1=1.0 / D, scalar2=None, op0=AL.mult))
        p.do("vector", lambda e: e.scalar_tensor_tensor(
            out=st_t2, in0=st_m, scalar=-1.0, in1=st_m, op0=AL.mult, op1=AL.mult))
        p.do("vector", lambda e: e.tensor_scalar(
            out=st_t1, in0=st_t1, scalar1=LN_EPS, scalar2=None, op0=AL.add))
        p.do("vector", lambda e: e.tensor_copy(out=junk[:, 0:1], in_=st_m))
        vv = p.do("vector", lambda e: e.tensor_tensor(
            out=st_d, in0=st_t1, in1=st_t2, op=AL.add), "v")
        p.wait("scalar", "v", vv)
        p.do("scalar", lambda e: e.activation(out=st_t1, in_=st_d, func=AF.Ln))
        p.do("scalar", lambda e: e.activation(out=st_t2, in_=st_d, func=AF.Ln))
        sv = p.do("scalar", lambda e: e.activation(
            out=st_r, in_=st_t1, func=AF.Exp, scale=-0.5), "s")
        p.wait("vector", "s", sv)
        p.do("vector", lambda e: e.scalar_tensor_tensor(
            out=st_t1, in0=st_r, scalar=1.0, in1=st_r, op0=AL.mult, op1=AL.mult))
        p.do("vector", lambda e: e.tensor_copy(out=junk[:, 0:1], in_=st_m))
        p.do("vector", lambda e: e.tensor_copy(out=junk[:, 8:9], in_=st_m))
        p.do("vector", lambda e: e.scalar_tensor_tensor(
            out=st_t1, in0=st_t1, scalar=-0.5, in1=st_d, op0=AL.mult, op1=AL.mult))
        p.do("vector", lambda e: e.tensor_copy(out=junk[:, 0:1], in_=st_m))
        p.do("vector", lambda e: e.tensor_copy(out=junk[:, 8:9], in_=st_m))
        p.do("vector", lambda e: e.tensor_scalar(
            out=st_t1, in0=st_t1, scalar1=1.5, scalar2=None, op0=AL.add))
        p.do("vector", lambda e: e.tensor_copy(out=junk[:, 0:1], in_=st_m))
        p.do("vector", lambda e: e.tensor_copy(out=junk[:, 8:9], in_=st_m))
        p.do("vector", lambda e: e.tensor_tensor(out=st_r, in0=st_r, in1=st_t1, op=AL.mult))
        p.do("vector", lambda e: e.tensor_copy(out=junk[:, 0:1], in_=st_m))
        p.do("vector", lambda e: e.tensor_copy(out=junk[:, 8:9], in_=st_m))
        wv = p.do("vector", lambda e, xb=xb, xo=xo: e.tensor_scalar(
            out=xo, in0=xb, scalar1=st_m, scalar2=st_r, op0=AL.subtract, op1=AL.mult), "v")
        p.wait("sync", "v", wv)
        state["st"][i % 2] = p.dma(
            "sync", lambda e, xo=xo, dst=dst: e.dma_start(out=dst, in_=xo), "st")

    for i in range(NBLK):
        ln1_tile(c.x_seq[i * 128:(i + 1) * 128, :], c.xn_tok[i * 128:(i + 1) * 128, :])
    for i in range(NOWN):
        ln1_tile(c.x_own[i * 128:(i + 1) * 128, :], c.xn_own[i * 128:(i + 1) * 128, :])

    brow = m.bufB[0:1, 4096:4096 + NFF]
    ones_s = m.small[0:1, 160:288]
    onescol_s = sm[:, 288:289]
    lv0 = p.dma("sync", lambda e: e.dma_start(out=ones_s, in_=c.ones_row_f))
    lv1 = p.dma("sync", lambda e: e.dma_start(out=onescol_s, in_=c.ones_col_f))
    pool = PsumPool(p, m.psb[0:4])

    def bcast_row(vec_dram, n, out_dram, off):
        lv = p.dma("sync", lambda e, n=n, v=vec_dram: e.dma_start(
            out=brow[:, 0:n], in_=v.unsqueeze(0)))
        p.wait("tensor", "ld", max(lv, lv0))
        last = None
        for ci in range(n // 512):
            bi, bank = pool.acquire()
            tv = p.do("tensor", lambda e, ci=ci, bank=bank: e.matmul(
                bank[:, :], ones_s, brow[:, ci * 512:(ci + 1) * 512],
                start=True, stop=True), "t")
            p.wait("vector", "t", tv)
            last = p.do("vector", lambda e, bank=bank, ci=ci, off=off: e.tensor_copy(
                out=m.bufC[:, off + ci * 512: off + (ci + 1) * 512], in_=bank[:, :]), "v")
            pool.released(bi, last)
        p.wait("sync", "v", last)
        return p.dma("sync", lambda e, n=n, o=out_dram, off=off: e.dma_start(
            out=o, in_=m.bufC[:, off:off + n]), "st")

    st_bsyn = bcast_row(c.b_syn, NFF, c.bsyn128[:, :], 0)
    p.wait("vector", "st", st_bsyn)
    bcast_row(c.b_out, D, c.cst128[0], 0)
    bcast_row(c.gamma, D, c.cst128[1], 2048)
    bcast_row(c.beta, D, c.cst128[2], 4096)

    absacc = sm[:, 16:144]
    sink2k = m.bufA[:, 6144:8192]
    sink8k = m.bufA[:, 8192:16384]
    wt2k = [m.bufB[:, 0:2048], m.bufB[:, 2048:4096]]
    col = 0
    s_done = [0, 0]
    last_s = None
    for i in range(NFF // 128):
        wb = wt2k[i % 2]
        p.wait("sync", "s", s_done[i % 2])
        lv = p.dma("sync", lambda e, i=i, wb=wb: e.dma_start(
            out=wb, in_=c.W_syn[i * 128:(i + 1) * 128, :]))
        p.wait("scalar", "ld", lv)
        last_s = p.do("scalar", lambda e, wb=wb, col=col: e.activation(
            out=sink2k, in_=wb, func=AF.Abs, accum_out=absacc[:, col:col + 1]), "s")
        s_done[i % 2] = last_s
        col += 1
    p.wait("vector", "s", last_s)
    p.do("vector", lambda e: e.tensor_scalar(
        out=junk[:, 0:64], in0=absacc[:, 0:64], scalar1=0.0, scalar2=None,
        op0=AL.add, op1=AL.add, accum_out=sm[:, 8:9]))

    wt8k = [m.bufB[:, 0:8192], m.bufB[:, 8192:16384]]
    s_done = [last_s, last_s]
    for i in range(D // 128):
        wb = wt8k[i % 2]
        p.wait("sync", "s", s_done[i % 2])
        lv = p.dma("sync", lambda e, i=i, wb=wb: e.dma_start(
            out=wb, in_=c.W_out[i * 128:(i + 1) * 128, :]))
        p.wait("scalar", "ld", lv)
        last_s = p.do("scalar", lambda e, wb=wb, col=col: e.activation(
            out=sink8k, in_=wb, func=AF.Abs, accum_out=absacc[:, col:col + 1]), "s")
        s_done[i % 2] = last_s
        col += 1
    p.wait("vector", "s", last_s)
    vv = p.do("vector", lambda e: e.tensor_scalar(
        out=junk[:, 0:16], in0=absacc[:, 64:80], scalar1=0.0, scalar2=None,
        op0=AL.add, op1=AL.add, accum_out=sm[:, 9:10]), "v")

    p.wait("tensor", "v", vv)
    p.wait("tensor", "ld", lv1)
    bank = m.psb[6]
    p.do("tensor", lambda e, bank=bank: e.matmul(
        bank[0:1, 0:1], sm[:, 8:9], onescol_s, start=True, stop=False))
    tv = p.do("tensor", lambda e, bank=bank: e.matmul(
        bank[0:1, 1:2], sm[:, 9:10], onescol_s, start=False, stop=True), "t")
    p.wait("vector", "t", tv)
    srow = sm[0:1, 144:152]
    p.do("vector", lambda e, bank=bank: e.tensor_scalar(
        out=srow[:, 0:2], in0=bank[0:1, 0:2], scalar1=0.5 / (NFF * D),
        scalar2=None, op0=AL.mult))
    p.do("vector", lambda e: e.tensor_copy(out=junk[:, 0:1], in_=sm[:, 0:1]))
    p.do("vector", lambda e: e.tensor_copy(out=junk[:, 8:9], in_=sm[:, 0:1]))
    vv = p.do("vector", lambda e: e.tensor_scalar(
        out=srow[:, 2:4], in0=srow[:, 0:2], scalar1=0.5e-8, scalar2=None,
        op0=AL.add), "v")
    p.wait("tensor", "v", vv)
    tv = p.do("tensor", lambda e, bank=bank: e.matmul(
        bank[:, 4:8], ones_s, srow[:, 0:4], start=True, stop=True), "t")
    p.wait("vector", "t", tv)
    vv = p.do("vector", lambda e, bank=bank: e.tensor_copy(
        out=sm[:, 148:152], in_=bank[:, 4:8]), "v")
    p.wait("sync", "v", vv)
    p.dma("sync", lambda e: e.dma_start(out=c.scal128[:, :], in_=sm[:, 148:152]), "st")
    p.wait("sync", "st", p.cur("st"))


def build_program(debug_taps=(), phases=("p0",)):
    nc = bass.Bass("TRN2", target_bir_lowering=False, debug=False)
    c = declare(nc, debug_taps)
    import contextlib
    with contextlib.ExitStack() as stack:
        m = Ctx()
        m.bufA = stack.enter_context(nc.sbuf_tensor("bufA", [128, 49152], BF16))
        m.bufB = stack.enter_context(nc.sbuf_tensor("bufB", [128, 16384], F32))
        m.bufC = stack.enter_context(nc.sbuf_tensor("bufC", [128, 8192], F32))
        m.small = stack.enter_context(nc.sbuf_tensor("small", [128, 3072], F32))
        m.psb = [stack.enter_context(nc.psum_tensor(f"pb{i}", [128, 512], F32))
                 for i in range(8)]
        p = Plan(nc, ["ld", "lds", "ldg", "t", "v", "s", "g", "st"])
        if "p0" in phases:
            phase0(p, c, m)
        if "p1" in phases:
            phase1(p, c, m)
        if "p2" in phases:
            phase2(p, c, m)
        if "p3" in phases:
            phase3(p, c, m)
        with nc.Block() as block:
            p.replay(block)
        for ctx in p.ctxs:
            ctx.__exit__(None, None, None)
    return nc


def phase1(p, c, m):
    p.wait("gpsimd", "v", p.cur("v"))
    p.wait("gpsimd", "s", p.cur("s"))
    p.wait("scalar", "st", p.cur("st"))
    castbuf = [m.bufA[:, 0:4096], m.bufA[:, 4096:8192]]
    g_done = [0, 0]
    for i in range(48):
        cb = castbuf[i % 2][:, 0:2048]
        p.wait("gpsimd", "ldg", g_done[i % 2])
        lv = p.do("gpsimd", lambda e, i=i, cb=cb: e.dma_start(
            out=cb, in_=c.w_qkv[i * 128:(i + 1) * 128, :]), "ldg", 16)
        p.wait("gpsimd", "ldg", lv)
        g_done[i % 2] = p.do("gpsimd", lambda e, i=i, cb=cb: e.dma_start(
            out=c.wqkv_bf[i * 128:(i + 1) * 128, :], in_=cb), "ldg", 16)
    for i in range(16):
        cb = castbuf[i % 2][:, 0:2048]
        p.wait("gpsimd", "ldg", g_done[i % 2])
        lv = p.do("gpsimd", lambda e, i=i, cb=cb: e.dma_start(
            out=cb, in_=c.w_o[i * 128:(i + 1) * 128, :]), "ldg", 16)
        p.wait("gpsimd", "ldg", lv)
        g_done[i % 2] = p.do("gpsimd", lambda e, i=i, cb=cb: e.dma_start(
            out=c.wo_bf[i * 128:(i + 1) * 128, :], in_=cb), "ldg", 16)
    wq_ready = g_done[0] if g_done[0] > g_done[1] else g_done[1]

    wt = [m.bufA[:, i * 2048:(i + 1) * 2048] for i in range(16)]
    xnf_all = m.bufB.bitcast(BF16)
    xnf = [xnf_all[:, i * 2048:(i + 1) * 2048] for i in range(16)]
    stage = [m.bufA[:, 32768 + i * 512: 32768 + (i + 1) * 512] for i in range(8)]
    stage_rel = [0] * 8
    pool = PsumPool(p, m.psb, consumer_sem="s")

    def load_weights(row0):
        vals = []
        for dc in range(16):
            lv = p.do("scalar", lambda e, dc=dc, row0=row0: e.dma_start(
                out=wt[dc], in_=c.wqkv_bf[row0:row0 + 2048, dc * 128:(dc + 1) * 128],
                transpose=True), "lds", 16)
            vals.append(lv)
        return vals

    def load_xnf(src, t0):
        vals = []
        for dc in range(16):
            lv = p.do("scalar", lambda e, dc=dc, src=src, t0=t0: e.dma_start(
                out=xnf[dc], in_=src[t0:t0 + 2048, dc * 128:(dc + 1) * 128],
                transpose=True), "lds", 16)
            vals.append(lv)
        return vals

    def run_fpass(weight_row0, xn_src, xn_len, store_fn, pre_wait_ldg=0):
        if pre_wait_ldg:
            p.wait("scalar", "ldg", pre_wait_ldg)
        wv = load_weights(weight_row0)
        for tg in range(xn_len // 2048):
            xv = load_xnf(xn_src, tg * 2048)
            units = [(oc, tc) for oc in range(16) for tc in range(4)]
            for u0 in range(0, len(units), 2):
                pair = units[u0:u0 + 2]
                acq = [pool.acquire() for _ in pair]
                tvals = [None] * len(pair)
                p.wait("tensor", "lds", max(max(wv), max(xv)))
                for dc in range(16):
                    for j, (oc, tc) in enumerate(pair):
                        bi, bank = acq[j]
                        inc = "t" if dc == 15 else None
                        v = p.do("tensor",
                            lambda e, dc=dc, oc=oc, tc=tc, bank=bank: e.matmul(
                                bank[:, :], wt[dc][:, oc * 128:(oc + 1) * 128],
                                xnf[dc][:, tc * 512:(tc + 1) * 512],
                                start=(dc == 0), stop=(dc == 15)), inc)
                        if v is not None:
                            tvals[j] = v
                for j, (oc, tc) in enumerate(pair):
                    bi, bank = acq[j]
                    p.wait("scalar", "t", tvals[j])
                    p.wait("scalar", "st", stage_rel[bi])
                    stg = stage[bi]
                    sv = p.do("scalar", lambda e, bank=bank, stg=stg: e.activation(
                        out=stg, in_=bank[:, :], func=AF.Copy), "s")
                    pool.released(bi, sv)
                    p.wait("sync", "s", sv)
                    store_fn(oc, tg, tc, stg)
                    stage_rel[bi] = p.cur("st")

    def q_store(oc, tg, tc, stg):
        p.dma("sync", lambda e, oc=oc, tc=tc, stg=stg: e.dma_start(
            out=c.qf[oc * 128:(oc + 1) * 128, tc * 512:(tc + 1) * 512], in_=stg), "st")

    def k_store(oc, tg, tc, stg):
        col = tg * 2048 + tc * 512
        p.dma("sync", lambda e, oc=oc, col=col, stg=stg: e.dma_start(
            out=c.kf[oc * 128:(oc + 1) * 128, col:col + 512], in_=stg), "st")

    run_fpass(0, c.xn_own, OWN, q_store, pre_wait_ldg=p.cur("ldg"))
    run_fpass(2048, c.xn_tok, S, k_store)

    wv = load_weights(4096)
    for tg in range(2):
        xv = load_xnf(c.xn_tok, tg * 2048)
        units = [(tk, vo) for tk in range(16) for vo in range(4)]
        for u0 in range(0, len(units), 2):
            pair = units[u0:u0 + 2]
            acq = [pool.acquire() for _ in pair]
            tvals = [None] * len(pair)
            p.wait("tensor", "lds", max(max(wv), max(xv)))
            for dc in range(16):
                for j, (tk, vo) in enumerate(pair):
                    bi, bank = acq[j]
                    inc = "t" if dc == 15 else None
                    v = p.do("tensor",
                        lambda e, dc=dc, tk=tk, vo=vo, bank=bank: e.matmul(
                            bank[:, :], xnf[dc][:, tk * 128:(tk + 1) * 128],
                            wt[dc][:, vo * 512:(vo + 1) * 512],
                            start=(dc == 0), stop=(dc == 15)), inc)
                    if v is not None:
                        tvals[j] = v
            for j, (tk, vo) in enumerate(pair):
                bi, bank = acq[j]
                tv = tvals[j]
                p.wait("scalar", "t", tv)
                p.wait("scalar", "st", stage_rel[bi])
                stg = stage[bi]
                sv = p.do("scalar", lambda e, bank=bank, stg=stg: e.activation(
                    out=stg, in_=bank[:, :], func=AF.Copy), "s")
                pool.released(bi, sv)
                p.wait("sync", "s", sv)
                t0 = tg * 2048 + tk * 128
                p.dma("sync", lambda e, t0=t0, vo=vo, stg=stg: e.dma_start(
                    out=c.vt[vo * 4:(vo + 1) * 4, t0:t0 + 128, :].transpose([1, 0, 2]),
                    in_=stg.rearrange("p (h d) -> p h d", h=4)), "st")
                stage_rel[bi] = p.cur("st")
    p.wait("sync", "st", p.cur("st"))


def phase2(p, c, m):
    sm = m.small
    ST = {n: sm[:, i:i + 1] for i, n in enumerate(
        ["lo", "hi", "chi", "negv", "cg", "v", "pred", "t1", "t2", "sz", "sz2",
         "mu", "var", "sb", "neglo", "vstar", "cnt", "gap"])}
    iota16 = sm[:, 32:48]
    konst = sm[:, 48:52]
    scalv = sm[:, 52:56]
    top16 = sm[:, 56:72]
    oh16 = sm[:, 72:88]
    negtau = sm[:, 88:90]
    junk2 = sm[:, 90:92]
    bsl = sm[:, 1024:2048]

    lv1 = p.dma("sync", lambda e: e.dma_start(out=iota16, in_=c.iota16))
    lv2 = p.dma("sync", lambda e: e.dma_start(out=konst, in_=c.konst))
    lv3 = p.dma("sync", lambda e: e.dma_start(out=scalv, in_=c.scal128[:, :]))
    p.wait("vector", "ld", max(lv1, lv2, lv3))
    p.do("vector", lambda e: e.tensor_scalar(
        out=negtau, in0=scalv[:, 2:4], scalar1=-1.0, scalar2=None, op0=AL.mult))

    wbufs = [m.bufB[:, 0:8192], m.bufB[:, 8192:16384]]
    sg1 = m.bufA[:, 0:8192]
    sg2 = m.bufA[:, 8192:16384]
    tstage = m.bufA[:, 16384:24576]
    vrel = [0, 0]
    srel = [0, 0]

    def tbuild(tiles, src_fn, dst_fn, width, tau_col):
        stv = 0
        for i in range(tiles):
            wb = wbufs[i % 2][:, 0:width]
            p.wait("sync", "s", srel[i % 2])
            lv = p.dma("sync", lambda e, wb=wb, i=i, src_fn=src_fn: e.dma_start(
                out=wb, in_=src_fn(i)))
            p.wait("scalar", "ld", lv)
            p.wait("scalar", "v", vrel[i % 2])
            p.do("scalar", lambda e, wb=wb, width=width, tau_col=tau_col:
                 e.activation(out=sg1[:, 0:width], in_=wb, func=AF.Sign,
                              bias=negtau[:, tau_col:tau_col + 1]))
            sv = p.do("scalar", lambda e, wb=wb, width=width, tau_col=tau_col:
                      e.activation(out=sg2[:, 0:width], in_=wb, func=AF.Sign,
                                   bias=scalv[:, 2 + tau_col:3 + tau_col]), "s")
            srel[i % 2] = sv
            p.wait("vector", "s", sv)
            vv = p.do("vector", lambda e, width=width: e.tensor_tensor(
                out=tstage[:, 0:width], in0=sg1[:, 0:width], in1=sg2[:, 0:width],
                op=AL.add), "v")
            vrel[i % 2] = vv
            p.wait("sync", "v", vv)
            stv = p.dma("sync", lambda e, i=i, width=width, dst_fn=dst_fn:
                        e.dma_start(out=dst_fn(i), in_=tstage[:, 0:width]), "st")
        return stv

    tsyn_st = tbuild(64, lambda i: c.W_syn[i * 128:(i + 1) * 128, :],
                     lambda i: c.tsyn[i * 128:(i + 1) * 128, :], 2048, 0)
    tout_st = tbuild(16, lambda i: c.W_out[i * 128:(i + 1) * 128, :],
                     lambda i: c.tout[i * 128:(i + 1) * 128, :], NFF, 1)
    tb_s_done = max(srel)
    tb_v_done = max(vrel)

    TT = [[m.bufA[:, (par * 16 + dc) * 1024:(par * 16 + dc + 1) * 1024]
           for dc in range(16)] for par in range(2)]
    xnfg = [m.bufA[:, 32768 + dc * 1024: 32768 + (dc + 1) * 1024]
            for dc in range(16)]
    hT = [m.bufA[:, n * 512:(n + 1) * 512] for n in range(64)]
    ToutT = [m.bufA[:, 32768 + j * 1024: 32768 + (j + 1) * 1024] for j in range(2)]
    xnu = [m.bufA[:, 36864 + t * 2048: 36864 + (t + 1) * 2048] for t in range(4)]
    gbuf = m.bufA.bitcast(F32)[:, 22528:24576]
    zbuf = m.bufB[:, 0:8192]
    zsbuf = m.bufB[:, 8192:16384]
    hbuf = m.bufB[:, 8192:16384].bitcast(BF16)[:, 0:8192]
    sink = hbuf
    utok = m.bufC[:, 0:2048]
    gam_bc = m.bufC[:, 2048:4096]
    bet_bc = m.bufC[:, 4096:6144]
    bout_bc = m.bufC[:, 6144:8192]

    lvg = p.dma("sync", lambda e: e.dma_start(out=gam_bc, in_=c.cst128[1]))
    lvb = p.dma("sync", lambda e: e.dma_start(out=bet_bc, in_=c.cst128[2]))
    lvo = p.dma("sync", lambda e: e.dma_start(out=bout_bc, in_=c.cst128[0]))
    p.wait("vector", "ld", max(lvg, lvb, lvo))

    pool = PsumPool(p, m.psb, consumer_sem="v")
    z_ready = {}
    h_ready = {}
    b1_last_t = {"v": 0}
    zstg = [sm[:, 2048 + i * 512: 2048 + (i + 1) * 512] for i in range(2)]
    zstg_rel = [0, 0]
    guards = {"bsl": 0, "tt": [0, 0], "b2_vv": 0, "b2_hst": 0, "g_st": 0,
              "b3_t": 0}

    def load_xnfg(g):
        return [p.dma("sync", lambda e, dc=dc, g=g: e.dma_start(
            out=xnfg[dc],
            in_=c.xn_own[g * 1024:(g + 1) * 1024, dc * 128:(dc + 1) * 128],
            transpose=True)) for dc in range(16)]

    def b1_eighth(g, e8, xnv, tt_wait_extra=0):
        par = e8 % 2
        p.wait("sync", "v", guards["bsl"])
        bv = p.dma("sync", lambda e, e8=e8: e.dma_start(
            out=bsl, in_=c.bsyn128[:, e8 * 1024:(e8 + 1) * 1024]))
        p.wait("sync", "t", guards["tt"][par])
        p.wait("sync", "st", tsyn_st)
        ttv = [p.dma("sync", lambda e, dc=dc, e8=e8, par=par: e.dma_start(
            out=TT[par][dc],
            in_=c.tsyn[e8 * 1024:(e8 + 1) * 1024, dc * 128:(dc + 1) * 128],
            transpose=True)) for dc in range(16)]
        p.wait("tensor", "ld", max(max(ttv), max(xnv), tt_wait_extra))
        for tk in range(8):
            acq = [pool.acquire() for _ in range(2)]
            tv = [None, None]
            for dc in range(16):
                for j in range(2):
                    bi, bank = acq[j]
                    vv = p.do("tensor", lambda e, dc=dc, tk=tk, j=j, par=par,
                              bank=bank: e.matmul(
                        bank[:, :], xnfg[dc][:, tk * 128:(tk + 1) * 128],
                        TT[par][dc][:, j * 512:(j + 1) * 512],
                        start=(dc == 0), stop=(dc == 15)),
                        "t" if dc == 15 else None)
                    if vv is not None:
                        tv[j] = vv
            b1_last_t["v"] = max(tv)
            guards["tt"][par] = max(tv)
            for j in range(2):
                bi, bank = acq[j]
                p.wait("vector", "t", tv[j])
                if tk == 0 and j == 0:
                    p.wait("vector", "ld", bv)
                p.wait("vector", "st", zstg_rel[j])
                stg = zstg[j]
                col = e8 * 1024 + j * 512
                row = g * 1024 + tk * 128
                vv = p.do("vector", lambda e, bank=bank, j=j, stg=stg:
                          e.scalar_tensor_tensor(
                    out=stg, in0=bank[:, :], scalar=scalv[:, 0:1],
                    in1=bsl[:, j * 512:(j + 1) * 512], op0=AL.mult, op1=AL.add), "v")
                pool.released(bi, vv)
                guards["bsl"] = vv
                p.wait("sync", "v", vv)
                sv = p.dma("sync", lambda e, row=row, col=col, stg=stg: e.dma_start(
                    out=c.z_sp[row:row + 128, col:col + 512],
                    in_=stg), "st")
                zstg_rel[j] = sv
                z_ready[(g, tk)] = sv

    _gapi = [0]
    gapcols = [sm[:, 92 + i:93 + i] for i in range(4)]

    def gapop(e):
        _gapi[0] += 1
        return e.tensor_copy(out=gapcols[_gapi[0] % 4], in_=konst[:, 1:2])

    def vts(out, in0, s1, s2=None, op0=AL.mult, op1=None, accum=None):
        def f(e):
            kw = {}
            if accum is not None:
                kw["accum_out"] = accum
            if op1 is not None:
                return e.tensor_scalar(out=out, in0=in0, scalar1=s1, scalar2=s2,
                                       op0=op0, op1=op1, **kw)
            return e.tensor_scalar(out=out, in0=in0, scalar1=s1, scalar2=None,
                                   op0=op0, **kw)
        return f

    def b2_tile(g, tk):
        row = g * 1024 + tk * 128
        p.wait("sync", "st", z_ready[(g, tk)])
        p.wait("sync", "v", max(guards["b2_vv"], tb_v_done))
        p.wait("sync", "s", tb_s_done)
        zlv = p.dma("sync", lambda e, row=row: e.dma_start(
            out=zbuf, in_=c.z_sp[row:row + 128, :]))
        p.wait("vector", "ld", zlv)
        p.wait("vector", "st", guards["b2_hst"])
        p.wait("scalar", "ld", zlv)
        p.wait("scalar", "st", guards["b2_hst"])
        p.do("vector", lambda e: e.tensor_scalar(
            out=zsbuf, in0=zbuf, scalar1=0.0, scalar2=None, op0=AL.add,
            op1=AL.add, accum_out=ST["sz"]))
        sv = p.do("scalar", lambda e: e.activation(
            out=sink, in_=zbuf, func=AF.Square, accum_out=ST["sz2"]), "s")
        p.wait("vector", "s", sv)
        p.do("vector", vts(ST["mu"], ST["sz"], 1.0 / NFF))
        p.do("vector", vts(ST["t1"], ST["sz2"], 1.0 / NFF))
        p.do("vector", lambda e: e.scalar_tensor_tensor(
            out=ST["t2"], in0=ST["mu"], scalar=-1.0, in1=ST["mu"],
            op0=AL.mult, op1=AL.mult))
        p.do("vector", gapop)
        p.do("vector", lambda e: e.tensor_tensor(
            out=ST["var"], in0=ST["t1"], in1=ST["t2"], op=AL.add))
        p.do("vector", gapop)
        p.do("vector", gapop)
        p.do("vector", vts(ST["sb"], ST["var"], 1.0, 1.6016, op0=AL.add,
                           op1=AL.mult))
        p.do("vector", gapop)
        p.do("vector", gapop)
        p.do("vector", vts(ST["sb"], ST["sb"], 1e-6, op0=AL.add))
        p.do("vector", gapop)
        p.do("vector", gapop)
        p.do("vector", lambda e: e.tensor_tensor(
            out=ST["hi"], in0=ST["mu"], in1=ST["sb"], op=AL.add))
        p.do("vector", lambda e: e.tensor_tensor(
            out=ST["lo"], in0=ST["mu"], in1=ST["sb"], op=AL.subtract))
        vv = p.do("vector", vts(ST["negv"], ST["hi"], -1.0), "v")
        for it in range(J0 + 1):
            p.wait("scalar", "v", vv)
            sv = p.do("scalar", lambda e: e.activation(
                out=sink, in_=zbuf, func=AF.Sign, bias=ST["negv"],
                accum_out=ST["cnt"]), "s")
            p.wait("vector", "s", sv)
            if it == 0:
                p.do("vector", vts(ST["chi"], ST["cnt"], float(NFF), 0.5,
                                   op0=AL.add, op1=AL.mult))
                p.do("vector", gapop)
                vv = p.do("vector", lambda e: e.scalar_tensor_tensor(
                    out=ST["negv"], in0=ST["lo"], scalar=ST["hi"],
                    in1=konst[:, 0:1], op0=AL.add, op1=AL.mult), "v")
                continue
            p.do("vector", vts(ST["cg"], ST["cnt"], float(NFF), 0.5,
                               op0=AL.add, op1=AL.mult))
            p.do("vector", vts(ST["v"], ST["negv"], -1.0))
            p.do("vector", vts(ST["pred"], ST["cg"], float(KTOP) - 0.25,
                               op0=AL.is_ge))
            p.do("vector", vts(ST["t2"], ST["cg"], float(KTOP) - 0.25,
                               op0=AL.is_lt))
            p.do("vector", gapop)
            pm = ST["pred"].bitcast(mybir.dt.uint32)
            nm = ST["t2"].bitcast(mybir.dt.uint32)
            p.do("vector", lambda e, pm=pm: e.copy_predicated(
                out=ST["lo"], mask=pm, data=ST["v"]))
            p.do("vector", lambda e, nm=nm: e.copy_predicated(
                out=ST["hi"], mask=nm, data=ST["v"]))
            p.do("vector", lambda e, nm=nm: e.copy_predicated(
                out=ST["chi"], mask=nm, data=ST["cg"]))
            vv = p.do("vector", lambda e: e.scalar_tensor_tensor(
                out=ST["negv"], in0=ST["lo"], scalar=ST["hi"], in1=konst[:, 0:1],
                op0=AL.add, op1=AL.mult), "v")
        p.do("vector", vts(ST["neglo"], ST["lo"], -1.0))
        vv = p.do("vector", gapop, "v")
        p.wait("scalar", "v", vv)
        sv = p.do("scalar", lambda e: e.activation(
            out=zsbuf, in_=zbuf, func=AF.Identity, bias=ST["neglo"]), "s")
        p.wait("vector", "s", sv)
        p.do("vector", lambda e: e.scalar_tensor_tensor(
            out=zsbuf, in0=zbuf, scalar=ST["hi"], in1=zsbuf,
            op0=AL.is_le, op1=AL.mult))
        p.do("vector", lambda e: e.max(out=top16[:, 0:8], in_=zsbuf))
        p.do("vector", gapop)
        p.do("vector", gapop)
        p.do("vector", lambda e: e.match_replace(
            out=zsbuf, in_to_replace=top16[:, 0:8], in_values=zsbuf,
            imm_value=-1.0e30))
        p.do("vector", lambda e: e.max(out=top16[:, 8:16], in_=zsbuf))
        p.do("vector", vts(ST["t1"], ST["chi"], -1.0, float(KTOP),
                           op0=AL.mult, op1=AL.add))
        p.do("vector", gapop)
        p.do("vector", gapop)
        p.do("vector", vts(oh16, iota16, ST["t1"], op0=AL.is_equal))
        p.do("vector", gapop)
        p.do("vector", gapop)
        p.do("vector", lambda e: e.tensor_tensor(
            out=oh16, in0=oh16, in1=top16, op=AL.mult))
        p.do("vector", gapop)
        p.do("vector", gapop)
        p.do("vector", vts(oh16, oh16, 0.0, op0=AL.add, op1=AL.add,
                           accum=ST["vstar"]))
        p.do("vector", gapop)
        p.do("vector", gapop)
        p.do("vector", lambda e: e.scalar_tensor_tensor(
            out=ST["vstar"], in0=ST["vstar"], scalar=ST["lo"], in1=ST["lo"],
            op0=AL.add, op1=AL.max))
        p.do("vector", gapop)
        p.do("vector", gapop)
        p.do("vector", vts(ST["vstar"], ST["vstar"], 0.0, op0=AL.max))
        p.do("vector", gapop)
        p.do("vector", gapop)
        vv = p.do("vector", lambda e: e.scalar_tensor_tensor(
            out=hbuf, in0=zbuf, scalar=ST["vstar"], in1=zbuf,
            op0=AL.is_gt, op1=AL.mult), "v")
        p.wait("sync", "v", vv)
        guards["b2_vv"] = vv
        if (g, tk) == (1, 7):
            p.dma("sync", lambda e: e.dma_start(
                out=c.dbg_small[:, :], in_=sm[:, 0:1024]), "st")
        hv = p.dma("sync", lambda e, row=row: e.dma_start(
            out=c.h_sp[row:row + 128, :], in_=hbuf), "st")
        h_ready[(g, tk)] = hv
        guards["b2_hst"] = hv

    def b3_sub(gsub):
        rows0 = gsub * 512
        g = gsub // 2
        tks = [gsub % 2 * 4 + i for i in range(4)]
        p.wait("sync", "st", max(h_ready[(g, tk)] for tk in tks))
        p.wait("sync", "t", max(b1_last_t["v"], guards["b3_t"]))
        hv = [p.dma("sync", lambda e, n=n, rows0=rows0: e.dma_start(
            out=hT[n], in_=c.h_sp[rows0:rows0 + 512, n * 128:(n + 1) * 128],
            transpose=True)) for n in range(64)]
        xv = [p.dma("sync", lambda e, t=t, rows0=rows0: e.dma_start(
            out=xnu[t], in_=c.xn_own[rows0 + t * 128:rows0 + (t + 1) * 128, :]))
            for t in range(4)]
        to_guard = [0, 0]
        for half in range(2):
            for i in range(8):
                p.wait("tensor", "v", pool.release[i])
            tv = [None] * 8
            for n in range(64):
                p.wait("sync", "t", to_guard[n % 2])
                tov = p.dma("sync", lambda e, n=n, half=half: e.dma_start(
                    out=ToutT[n % 2],
                    in_=c.tout[half * 1024:(half + 1) * 1024,
                               n * 128:(n + 1) * 128],
                    transpose=True))
                p.wait("tensor", "ld", tov)
                if n == 0 and half == 0:
                    p.wait("tensor", "ld", max(max(hv), max(xv)))
                    p.wait("tensor", "st", tout_st)
                for ts in range(4):
                    for dq in range(2):
                        bank = m.psb[ts * 2 + dq]
                        vv = p.do("tensor", lambda e, n=n, ts=ts, dq=dq,
                                  bank=bank, half=half: e.matmul(
                            bank[:, :], hT[n][:, ts * 128:(ts + 1) * 128],
                            ToutT[n % 2][:, dq * 512:(dq + 1) * 512],
                            start=(n == 0), stop=(n == 63)),
                            "t" if (n == 63 or (ts == 3 and dq == 1)) else None)
                        if vv is not None and n == 63:
                            tv[ts * 2 + dq] = vv
                        if vv is not None and ts == 3 and dq == 1:
                            to_guard[n % 2] = vv
                            guards["b3_t"] = vv
            for ts in range(4):
                for dq in range(2):
                    bi = ts * 2 + dq
                    bank = m.psb[bi]
                    cols = half * 1024 + dq * 512
                    p.wait("vector", "t", tv[bi])
                    p.do("vector", lambda e, bank=bank, ts=ts, cols=cols:
                         e.scalar_tensor_tensor(
                             out=utok[:, cols:cols + 512], in0=bank[:, :],
                             scalar=scalv[:, 1:2],
                             in1=bout_bc[:, cols:cols + 512],
                             op0=AL.mult, op1=AL.add))
                    p.do("vector", lambda e: e.tensor_copy(
                        out=gapcols[0], in_=konst[:, 1:2]))
                    vv = p.do("vector", lambda e, ts=ts, cols=cols:
                              e.tensor_tensor(
                                  out=ubuf[ts][:, cols:cols + 512],
                                  in0=utok[:, cols:cols + 512],
                                  in1=xnu[ts][:, cols:cols + 512], op=AL.add), "v")
                    pool.release[bi] = vv
        for ts in range(4):
            ln2_silu(gsub, ts)

    ubuf = xnu

    def ln2_silu(gsub, ts):
        u = ubuf[ts]
        rows0 = gsub * 512 + ts * 128
        p.do("vector", lambda e: e.tensor_scalar(
            out=utok, in0=u, scalar1=0.0, scalar2=None, op0=AL.add,
            op1=AL.add, accum_out=ST["sz"]))
        p.do("vector", lambda e: e.scalar_tensor_tensor(
            out=utok, in0=u, scalar=0.0, in1=u, op0=AL.add,
            op1=AL.mult, accum_out=ST["sz2"]))
        p.do("vector", vts(ST["mu"], ST["sz"], 1.0 / D))
        p.do("vector", vts(ST["t1"], ST["sz2"], 1.0 / D))
        p.do("vector", lambda e: e.scalar_tensor_tensor(
            out=ST["t2"], in0=ST["mu"], scalar=-1.0, in1=ST["mu"],
            op0=AL.mult, op1=AL.mult))
        p.do("vector", gapop)
        p.do("vector", lambda e: e.tensor_tensor(
            out=ST["var"], in0=ST["t1"], in1=ST["t2"], op=AL.add))
        p.do("vector", gapop)
        p.do("vector", gapop)
        vv = p.do("vector", vts(ST["t1"], ST["var"], LN_EPS, op0=AL.add), "v")
        p.wait("scalar", "v", vv)
        p.do("scalar", lambda e: e.activation(out=ST["t2"], in_=ST["t1"],
                                              func=AF.Ln))
        p.do("scalar", lambda e: e.activation(out=ST["gap"], in_=ST["t1"],
                                              func=AF.Ln))
        sv = p.do("scalar", lambda e: e.activation(
            out=ST["v"], in_=ST["t2"], func=AF.Exp, scale=-0.5), "s")
        p.wait("vector", "s", sv)
        p.do("vector", lambda e: e.scalar_tensor_tensor(
            out=ST["t2"], in0=ST["v"], scalar=1.0, in1=ST["v"],
            op0=AL.mult, op1=AL.mult))
        p.do("vector", gapop)
        p.do("vector", gapop)
        p.do("vector", lambda e: e.scalar_tensor_tensor(
            out=ST["t2"], in0=ST["t2"], scalar=-0.5, in1=ST["t1"],
            op0=AL.mult, op1=AL.mult))
        p.do("vector", gapop)
        p.do("vector", gapop)
        p.do("vector", vts(ST["t2"], ST["t2"], 1.5, op0=AL.add))
        p.do("vector", gapop)
        p.do("vector", gapop)
        p.do("vector", lambda e: e.tensor_tensor(
            out=ST["v"], in0=ST["v"], in1=ST["t2"], op=AL.mult))
        p.do("vector", gapop)
        p.do("vector", gapop)
        p.wait("vector", "st", guards["g_st"])
        p.do("vector", lambda e: e.tensor_scalar(
            out=gbuf, in0=u, scalar1=ST["mu"], scalar2=ST["v"],
            op0=AL.subtract, op1=AL.mult))
        p.do("vector", lambda e: e.tensor_tensor(
            out=gbuf, in0=gbuf, in1=gam_bc, op=AL.mult))
        vv = p.do("vector", lambda e: e.tensor_tensor(
            out=gbuf, in0=gbuf, in1=bet_bc, op=AL.add), "v")
        p.wait("scalar", "v", vv)
        sv = p.do("scalar", lambda e: e.activation(
            out=gbuf, in_=gbuf, func=AF.Silu), "s")
        p.wait("sync", "s", sv)
        guards["g_st"] = p.dma("sync", lambda e, rows0=rows0: e.dma_start(
            out=c.g_sp[rows0:rows0 + 128, :], in_=gbuf), "st")

    xnv0 = load_xnfg(0)
    for e8 in range(8):
        b1_eighth(0, e8, xnv0, tt_wait_extra=0)
    p.wait("sync", "t", b1_last_t["v"])
    xnv1 = load_xnfg(1)
    b2q = [(0, tk) for tk in range(8)] + [(1, tk) for tk in range(8)]
    b2i = 0
    for e8 in range(8):
        b1_eighth(1, e8, xnv1)
        if e8 >= 2 and b2i < 6:
            b2_tile(*b2q[b2i]); b2i += 1
    while b2i < 8:
        b2_tile(*b2q[b2i]); b2i += 1
    b3_sub(0)
    b2_tile(*b2q[b2i]); b2i += 1
    b2_tile(*b2q[b2i]); b2i += 1
    b3_sub(1)
    b2_tile(*b2q[b2i]); b2i += 1
    b2_tile(*b2q[b2i]); b2i += 1
    b3_sub(2)
    while b2i < 16:
        b2_tile(*b2q[b2i]); b2i += 1
    b3_sub(3)
    p.wait("sync", "st", p.cur("st"))


def phase3(p, c, m):
    sm = m.small
    slab = m.bufA[:, 0:32768]
    kbuf = [m.bufA[:, 32768 + i * 4096: 32768 + (i + 1) * 4096] for i in range(2)]
    vbuf = [m.bufA[:, 40960 + i * 4096: 40960 + (i + 1) * 4096] for i in range(2)]
    cmask = [m.bufB[:, i * 512:(i + 1) * 512] for i in range(9)]
    bufBb = m.bufB.bitcast(BF16)
    qbuf = [bufBb[:, 9216 + i * 2048: 9216 + (i + 1) * 2048] for i in range(2)]
    pt = [bufBb[:, 13312 + i * 512: 13312 + (i + 1) * 512] for i in range(4)]
    rln = m.bufB[0:1, 7680:8192]
    rrow = m.bufB[0:1, 8192:8704]
    ones_s = m.bufB[0:1, 8704:8832]
    onesc_bf = bufBb[:, 17664:17665]
    wot = [bufBb[:, i * 2048:(i + 1) * 2048] for i in range(16)]
    gsl = m.bufC[:, 0:2048]
    xsl = m.bufC[:, 2048:4096]
    osl = m.bufC[:, 4096:6144]
    rbc = m.bufC[:, 6144:6656]

    p.wait("sync", "v", p.cur("v"))
    p.wait("sync", "s", p.cur("s"))
    p.wait("sync", "t", p.cur("t"))
    p.wait("tensor", "v", p.cur("v"))
    p.wait("tensor", "s", p.cur("s"))
    mlv = [p.dma("sync", lambda e, i=i: e.dma_start(out=cmask[i], in_=c.cmask[i]))
           for i in range(9)]
    lvo = p.dma("sync", lambda e: e.dma_start(out=ones_s, in_=c.ones_row_f))
    lvc = p.dma("sync", lambda e: e.dma_start(out=onesc_bf, in_=c.ones_col_bf))
    p.wait("vector", "ld", max(mlv))

    kq_rel = [0, 0]
    sb_rel = [0, 0, 0, 0]
    pt_rel = [0, 0, 0, 0]
    att_rel = [0, 0]
    r6_rel = [0, 0, 0, 0]
    r7_rel = [0]
    uidx = [0]

    def head(h):
        par2 = h % 2
        p.wait("sync", "t", kq_rel[par2])
        kv = p.dma("sync", lambda e, h=h: e.dma_start(
            out=kbuf[par2], in_=c.kf[h * 128:(h + 1) * 128, :]))
        vv0 = p.dma("sync", lambda e, h=h: e.dma_start(
            out=vbuf[par2].rearrange("p (kc d) -> p kc d", d=128),
            in_=c.vt[h].rearrange("(kc p) d -> p kc d", p=128)))
        qv = p.dma("sync", lambda e, h=h: e.dma_start(
            out=qbuf[par2], in_=c.qf[h * 128:(h + 1) * 128, :]))
        ldv = max(kv, vv0, qv, lvc)
        first = True
        for qc in range(4):
            lim = min(8 * qc + 8, 31)
            abank = m.psb[2 + (qc % 2)]
            for kc in range(lim + 1):
                u = uidx[0]; uidx[0] += 1
                sb = u % 4
                sbank = m.psb[[0, 1, 4, 5][sb]]
                p.wait("tensor", "s", sb_rel[sb])
                if first:
                    p.wait("tensor", "ld", ldv)
                    first = False
                tv = p.do("tensor", lambda e, kc=kc, qc=qc, sbank=sbank:
                          e.matmul(sbank[:, :],
                                   kbuf[par2][:, kc * 128:(kc + 1) * 128],
                                   qbuf[par2][:, qc * 512:(qc + 1) * 512],
                                   start=True, stop=True), "t")
                r = kc - 8 * qc
                if 0 <= r <= 8:
                    p.wait("vector", "t", tv)
                    mv = p.do("vector", lambda e, r=r, sbank=sbank:
                              e.tensor_tensor(out=sbank[:, :], in0=sbank[:, :],
                                              in1=cmask[r], op=AL.add), "v")
                    p.wait("scalar", "v", mv)
                else:
                    p.wait("scalar", "t", tv)
                myp = u % 4
                p.wait("scalar", "t", pt_rel[myp])
                sv = p.do("scalar", lambda e, sbank=sbank, myp=myp: e.activation(
                    out=pt[myp], in_=sbank[:, :], func=AF.Exp,
                    scale=INV_SQRT_HD), "s")
                sb_rel[sb] = sv
                p.wait("tensor", "s", sv)
                if kc == 0:
                    p.wait("tensor", "v", att_rel[qc % 2])
                    p.wait("tensor", "s", max(r6_rel))
                p.do("tensor", lambda e, kc=kc, myp=myp, abank=abank, lim=lim:
                     e.matmul(abank[:, :],
                              vbuf[par2][:, kc * 128:(kc + 1) * 128], pt[myp],
                              start=(kc == 0), stop=(kc == lim)))
                tv2 = p.do("tensor", lambda e, myp=myp, kc=kc, lim=lim:
                           e.matmul(m.psb[6][0:1, :], onesc_bf, pt[myp],
                                    start=(kc == 0), stop=(kc == lim)), "t")
                pt_rel[myp] = tv2
            p.wait("scalar", "t", tv2)
            p.do("scalar", lambda e: e.activation(
                out=rln, in_=m.psb[6][0:1, :], func=AF.Ln))
            p.do("scalar", lambda e: e.activation(
                out=rrow, in_=m.psb[6][0:1, :], func=AF.Ln))
            sv = p.do("scalar", lambda e: e.activation(
                out=rrow, in_=rln, func=AF.Exp, scale=-1.0), "s")
            r6_rel[qc] = sv
            p.wait("tensor", "s", sv)
            p.wait("tensor", "ld", lvo)
            p.wait("tensor", "v", r7_rel[0])
            tv3 = p.do("tensor", lambda e: e.matmul(
                m.psb[7][:, :], ones_s, rrow, start=True, stop=True), "t")
            p.wait("vector", "t", tv3)
            if h == 0 and qc == 0 and DBG_ATT:
                p.do("vector", lambda e: e.tensor_copy(
                    out=m.bufC[:, 6656:7168], in_=abank[:, :]))
                p.do("vector", lambda e: e.tensor_copy(
                    out=m.bufC[0:1, 7168:7680], in_=rln))
                dv = p.do("vector", lambda e: e.tensor_copy(
                    out=m.bufC[0:1, 7680:8192], in_=rrow), "v")
                p.wait("sync", "v", dv)
                p.dma("sync", lambda e: e.dma_start(
                    out=c.dbg_att[:, 0:512], in_=m.bufC[:, 6656:7168]), "st")
                p.dma("sync", lambda e: e.dma_start(
                    out=c.dbg_att[0:1, 512:1024], in_=m.bufC[0:1, 7168:7680]), "st")
                p.dma("sync", lambda e: e.dma_start(
                    out=c.dbg_att[0:1, 1024:1536], in_=m.bufC[0:1, 7680:8192]), "st")
            p.do("vector", lambda e: e.tensor_copy(out=rbc, in_=m.psb[7][:, :]))
            p.do("vector", lambda e: e.tensor_copy(out=sm[:, 96:97], in_=sm[:, 48:49]))
            p.do("vector", lambda e: e.tensor_copy(out=sm[:, 97:98], in_=sm[:, 48:49]))
            vv2 = p.do("vector", lambda e, h=h, qc=qc, abank=abank: e.tensor_tensor(
                out=slab[:, h * 2048 + qc * 512: h * 2048 + (qc + 1) * 512],
                in0=abank[:, :], in1=rbc, op=AL.mult), "v")
            att_rel[qc % 2] = vv2
            r7_rel[0] = vv2
        kq_rel[par2] = p.cur("t")

    for h in range(P3_HEADS):
        head(h)

    if not P3_WO:
        for hh in range(16):
            vvx = p.do("vector", lambda e, hh=hh: e.tensor_copy(
                out=osl, in_=slab[:, hh * 2048:(hh + 1) * 2048]), "v")
            p.wait("sync", "v", vvx)
            sx = p.dma("sync", lambda e, hh=hh: e.dma_start(
                out=c.y[hh * 128:(hh + 1) * 128, :], in_=osl), "st")
            p.wait("vector", "st", sx)
        p.wait("sync", "st", p.cur("st"))
        p.wait("scalar", "st", p.cur("st"))
        p.wait("vector", "st", p.cur("st"))
        p.wait("tensor", "st", p.cur("st"))
        p.wait("gpsimd", "st", p.cur("st"))
        return

    p.wait("sync", "t", p.cur("t"))
    p.wait("sync", "s", p.cur("s"))
    p.wait("sync", "v", p.cur("v"))
    wlv = [p.dma("sync", lambda e, fc=fc: e.dma_start(
        out=wot[fc], in_=c.wo_bf[:, fc * 128:(fc + 1) * 128], transpose=True))
        for fc in range(16)]
    wo_rel = [0, 0, 0, 0]
    for qt in range(16):
        glv = p.dma("sync", lambda e, qt=qt: e.dma_start(
            out=gsl, in_=c.g_sp[qt * 128:(qt + 1) * 128, :]))
        xlv = p.dma("sync", lambda e, qt=qt: e.dma_start(
            out=xsl, in_=c.x_own[qt * 128:(qt + 1) * 128, :]))
        tvs = [None] * 4
        if qt == 0:
            p.wait("tensor", "ld", max(wlv))
        for fc in range(16):
            for oc in range(4):
                bank = m.psb[oc]
                if qt == 0 and fc == 0 and oc == 0:
                    p.wait("tensor", "v", p.cur("v"))
                    p.wait("tensor", "s", p.cur("s"))
                if fc == 0:
                    p.wait("tensor", "v", wo_rel[oc])
                vv = p.do("tensor", lambda e, fc=fc, oc=oc, qt=qt, bank=bank:
                          e.matmul(
                    bank[:, :],
                    slab[:, fc * 2048 + qt * 128: fc * 2048 + (qt + 1) * 128],
                    wot[fc][:, oc * 512:(oc + 1) * 512],
                    start=(fc == 0), stop=(fc == 15)),
                    "t" if fc == 15 else None)
                if vv is not None:
                    tvs[oc] = vv
        p.wait("vector", "ld", max(glv, xlv))
        for oc in range(4):
            bank = m.psb[oc]
            p.wait("vector", "t", tvs[oc])
            wo_rel[oc] = p.do("vector", lambda e, oc=oc, bank=bank: e.tensor_tensor(
                out=osl[:, oc * 512:(oc + 1) * 512], in0=bank[:, :],
                in1=gsl[:, oc * 512:(oc + 1) * 512], op=AL.mult), "v")
        p.do("vector", lambda e: e.tensor_copy(out=sm[:, 96:97], in_=sm[:, 48:49]))
        p.do("vector", lambda e: e.tensor_copy(out=sm[:, 97:98], in_=sm[:, 48:49]))
        vv = None
        for oc in range(4):
            vv = p.do("vector", lambda e, oc=oc: e.tensor_tensor(
                out=osl[:, oc * 512:(oc + 1) * 512],
                in0=osl[:, oc * 512:(oc + 1) * 512],
                in1=xsl[:, oc * 512:(oc + 1) * 512], op=AL.add), "v")
        p.wait("sync", "v", vv)
        p.dma("sync", lambda e, qt=qt: e.dma_start(
            out=c.y[qt * 128:(qt + 1) * 128, :], in_=osl), "st")
        p.wait("sync", "st", p.cur("st"))
    p.wait("sync", "st", p.cur("st"))
    p.wait("scalar", "st", p.cur("st"))
    p.wait("vector", "st", p.cur("st"))
    p.wait("tensor", "st", p.cur("st"))
    p.wait("gpsimd", "st", p.cur("st"))


def shard_inputs(x, w_qkv, w_o, W_syn, b_syn, W_out, b_out, gamma, beta):
    in_maps = []
    own_rows = {}
    for ci in range(NCORES):
        b, par = divmod(ci, 2)
        rows = np.concatenate([np.arange((2 * i + par) * 128, (2 * i + par + 1) * 128)
                               for i in range(NOWN)])
        own_rows[ci] = rows
        in_maps.append({
            "x_seq": np.ascontiguousarray(x[b]),
            "x_own": np.ascontiguousarray(x[b][rows]),
            "cmask": host_masks(par),
            "w_qkv": np.ascontiguousarray(w_qkv, dtype=np.float32),
            "w_o": np.ascontiguousarray(w_o, dtype=np.float32),
            "W_syn": np.ascontiguousarray(W_syn, dtype=np.float32),
            "b_syn": np.ascontiguousarray(b_syn, dtype=np.float32),
            "W_out": np.ascontiguousarray(W_out, dtype=np.float32),
            "b_out": np.ascontiguousarray(b_out, dtype=np.float32),
            "gamma": np.ascontiguousarray(gamma, dtype=np.float32),
            "beta": np.ascontiguousarray(beta, dtype=np.float32),
        })
    return in_maps, own_rows


ALL_PHASES = ("p0", "p1", "p2", "p3")


def kernel(L0_attn_hidden_states, L0_attn_w_qkv, L0_attn_w_o, L0_ffn_W_syn,
           L0_ffn_b_syn, L0_ffn_W_out, L0_ffn_b_out, L0_ffn_gamma, L0_ffn_beta):
    x = np.ascontiguousarray(L0_attn_hidden_states, dtype=np.float32)
    nc = build_program(phases=ALL_PHASES)
    in_maps, own_rows = shard_inputs(
        x, L0_attn_w_qkv, L0_attn_w_o, L0_ffn_W_syn, L0_ffn_b_syn,
        L0_ffn_W_out, L0_ffn_b_out, L0_ffn_gamma, L0_ffn_beta)
    res = run_bass_kernel_spmd(nc, in_maps, core_ids=list(range(NCORES)))
    out = np.empty((B, S, D), dtype=np.float32)
    for ci in range(NCORES):
        out[ci // 2][own_rows[ci]] = res.results[ci]["y"]
    return out
